# version 06
# baseline (speedup 1.0000x reference)
import numpy as np
import contextlib
import concourse.bass as bass
import concourse.mybir as mybir
from concourse.alu_op_type import AluOpType as OP
from concourse.bass_utils import run_bass_kernel_spmd

F32 = mybir.dt.float32
BF16 = mybir.dt.bfloat16
AF = mybir.ActivationFunctionType

D = 1024
G = 8
DI = 2048
GI = 16
CD = 2304
GC = 18
NH = 32
HD = 64
CH = 64
FP = 4384
NB = 2
T = 2048
TC = NB * T
SLAB = 512
NSLAB = TC // SLAB
SPB = T // SLAB
CPS = SLAB // CH
HSP = 21
TBS = SLAB // 128
EPS = 1e-5
ARENA_COLS = 50128
DBG = False
PASSES = {"stats", "AB", "Z", "C1", "C2", "F", "B"}
HACK = set()


class QOp:
    __slots__ = ("fn", "dma", "inc", "ch")

    def __init__(self, fn, dma):
        self.fn = fn
        self.dma = dma
        self.inc = None


class Emit:
    ENGINES = ("sync", "tensor", "vector", "scalar", "gpsimd")
    CHANNELS = ENGINES + ("gpdma",)

    def __init__(self):
        self.q = {e: [] for e in self.ENGINES}
        self.sem = {}
        self.cnt = {c: 0 for c in self.CHANNELS}
        self.dirty = {c: None for c in self.CHANNELS}
        self.water = {e: {} for e in self.ENGINES}

    def op(self, eng, fn, dma=False):
        qo = QOp(fn, dma)
        self.q[eng].append(qo)
        ch = "gpdma" if (eng == "gpsimd" and dma) else eng
        qo.ch = ch
        if dma:
            qo.inc = (ch, 16)
            self.cnt[ch] += 16
        else:
            self.dirty[ch] = qo
        return qo

    def force(self, ch):
        qo = self.dirty[ch]
        if qo is None:
            return
        qo.inc = (ch, 16 if qo.dma else 1)
        self.cnt[ch] += qo.inc[1]
        self.dirty[ch] = None

    def wait(self, eng, src, val=None):
        self.force(src)
        v = self.cnt[src] if val is None else val
        if v <= 0 or v <= self.water[eng].get(src, 0):
            return
        self.water[eng][src] = v
        self.q[eng].append(("wait", src, v))

    def sfence(self, eng):
        self.force(eng)
        v = self.cnt[eng]
        if v > self.water[eng].get(eng, 0):
            self.water[eng][eng] = v
            self.q[eng].append(("wait", eng, v))

    def fence(self, srcs, dsts):
        if not isinstance(srcs, (list, tuple)):
            srcs = [srcs]
        if not isinstance(dsts, (list, tuple)):
            dsts = [dsts]
        for s in srcs:
            for d in dsts:
                if d != s:
                    self.wait(d, s)

    def barrier(self):
        for c in self.CHANNELS:
            self.force(c)
        for d in self.ENGINES:
            for c in self.CHANNELS:
                if c != d:
                    self.wait(d, c)

    def mark(self, ch):
        self.force(ch)
        return self.cnt[ch]

    def replay(self, block):
        for eng in self.ENGINES:
            ops = self.q[eng]
            sems = self.sem

            def section(proxy, ops=ops, sems=sems):
                for item in ops:
                    if isinstance(item, QOp):
                        r = item.fn(proxy)
                        if item.inc is not None:
                            nm, amt = item.inc
                            r.then_inc(sems[nm], amt)
                    else:
                        _, src, v = item
                        proxy.wait_ge(sems[src], v)
            getattr(block, eng)(section)


class Arena:
    def __init__(self, tensor, start, end):
        self.t = tensor
        self.tb = tensor.bitcast(BF16)
        self.start = start
        self.end = end
        self.pos = start

    def _alloc(self, ncols_f32, align=8):
        p = (self.pos + align - 1) // align * align
        assert p + ncols_f32 <= self.end, \
            f"arena overflow: need {ncols_f32} at {p} end {self.end}"
        self.pos = p + ncols_f32
        return p

    def f32(self, ncols, parts=128, p0=0):
        p = self._alloc(ncols)
        return self.t[p0:p0 + parts, p:p + ncols]

    def bf16(self, ncols_bf16, parts=128, p0=0):
        p = self._alloc((ncols_bf16 + 1) // 2)
        return self.tb[p0:p0 + parts, 2 * p:2 * p + ncols_bf16]

    def save(self):
        return self.pos

    def reset(self, pos):
        self.pos = pos


def build_program(nsub=4):
    nc = bass.Bass("TRN2", target_bir_lowering=False, detect_race_conditions=False)

    def din(name, shape, dt=F32):
        return nc.dram_tensor(name, shape, dt, kind="ExternalInput").ap()

    def dint(name, shape, dt):
        return nc.dram_tensor(name, shape, dt, kind="Internal").ap()

    x_fm = din("x_fm", [D, TC])
    wt_in = din("wt_in", [D, FP])
    wt_out = din("wt_out", [DI, D])
    wt_i = din("wt_i", [D, D])
    wt_f = din("wt_f", [D, D])
    wt_g = din("wt_g", [D, D])
    wt_o = din("wt_o", [D, D])
    cn_w1 = din("cn_w1", [D, D])
    cn_w2 = din("cn_w2", [D, 512])
    cn_w3 = din("cn_w3", [512, 256])
    cn_w4 = din("cn_w4", [256, 1])
    gc_w1 = din("gc_w1", [D, 512])
    gc_w2 = din("gc_w2", [512, 256])
    gc_w3 = din("gc_w3", [256, 1])
    p_convw = din("p_convw", [128, 72])
    p_convb = din("p_convb", [128, 18])
    p_ndtb = din("p_ndtb", [32, 1])
    p_alog = din("p_alog", [32, 1])
    p_dskrow = din("p_dskrow", [1, DI])
    p_anw = din("p_anw", [128, 16])
    p_lng = din("p_lng", [128, 8])
    p_lnb = din("p_lnb", [128, 8])
    p_bnw = din("p_bnw", [128, 8])
    p_cnb1 = din("p_cnb1", [128, 8])
    p_cnb2 = din("p_cnb2", [128, 4])
    p_cnb3 = din("p_cnb3", [128, 2])
    p_gcb1 = din("p_gcb1", [128, 4])
    p_gcb2 = din("p_gcb2", [128, 2])
    p_sc3 = din("p_sc3", [1, 3])

    out_fm = nc.dram_tensor("out_fm", [D, TC], F32, kind="ExternalOutput").ap()

    h_x = dint("h_x", [D, TC], F32)
    h_xn = dint("h_xn", [D, TC], BF16)
    h_xbc = dint("h_xbc", [CD, TC], BF16)
    h_z = dint("h_z", [DI, TC], F32)
    h_yn = dint("h_yn", [DI, TC], BF16)
    h_xlf = dint("h_xlf", [D, TC], BF16)
    h_xlb = dint("h_xlb", [D, TC], BF16)
    h_hb = dint("h_hb", [D, TC], F32)
    if DBG:
        def dd(name, shape, dt):
            return nc.dram_tensor(name, shape, dt, kind="ExternalOutput").ap()
        hd_xs = dd("hd_xs", [CD, TC], BF16)
        hd_dt = dd("hd_dt", [32, TC], F32)
        hd_acs = dd("hd_acs", [32, TC], F32)
        hd_wst = dd("hd_wst", [32, TC], F32)
        hd_ym = dd("hd_ym", [DI, TC], F32)
        hd_s = dd("hd_s", [128, DI * NSLAB], F32)
        hd_dtr = dd("hd_dtr", [32, TC], F32)
        hd_sp = dd("hd_sp", [32, TC], F32)
        hd_mp = dd("hd_mp", [128, DI * TBS], BF16)
        hd_xhtm = dd("hd_xhtm", [128, DI * TBS], BF16)
        hd_srow = dd("hd_srow", [1, TC], F32)
        hd_qrow = dd("hd_qrow", [1, TC], F32)
        hd_arow = dd("hd_arow", [1, TC], BF16)
        hd_brow = dd("hd_brow", [1, TC], BF16)

    es = contextlib.ExitStack()
    ARN = es.enter_context(nc.sbuf_tensor("ARN", [128, ARENA_COLS], F32))
    ARNB = ARN.bitcast(BF16)
    CONB = es.enter_context(nc.sbuf_tensor("CONB", [128, 5120], BF16))
    SMALL = es.enter_context(nc.sbuf_tensor("SMALL", [128, 512], F32))
    PS = [es.enter_context(nc.psum_tensor(f"PS{i}", [128, 512], F32)) for i in range(7)]
    PT = es.enter_context(nc.psum_tensor("PT", [128, 1024], BF16))
    PTB2 = PS[2].bitcast(BF16)
    PTB3 = PS[3].bitcast(BF16)

    em = Emit()
    em.sem = {c: es.enter_context(nc.semaphore("sem_" + c)) for c in Emit.CHANNELS}

    IDN = CONB[:, 0:128]
    TRI2 = CONB[:, 128:192]
    SL2 = CONB[:, 192:320]
    ONESC = CONB[:, 320:448]
    MONES = CONB[:, 448:449]
    E2A = CONB[:, 452:516]
    I2 = CONB[:, 516:580]
    SELR = CONB[:, 580:580 + DI]
    DSKI = CONB[:, 2688:2688 + DI]

    def gp(fn):
        em.op("gpsimd", fn)

    def dve(fn):
        em.op("vector", fn)

    def act(fn):
        em.op("scalar", fn)

    def pe(fn):
        em.op("tensor", fn)

    def dma_in(dst, src):
        em.op("sync", lambda e, d=dst, s=src: e.dma_start(out=d, in_=s), dma=True)

    def dma_out(dst, src):
        em.op("gpsimd", lambda e, d=dst, s=src: e.dma_start(out=d, in_=s), dma=True)

    def dma_out_sync(dst, src):
        em.op("sync", lambda e, d=dst, s=src: e.dma_start(out=d, in_=s), dma=True)

    def SMc(c0, n, r0=0, r1=128):
        return SMALL[r0:r1, c0:c0 + n]

    C_convw, C_convb = 0, 72
    C_ndtb, C_expA = 90, 91
    C_anw, C_lng, C_lnb, C_bnw = 92, 108, 116, 124
    C_cnb1, C_cnb2, C_cnb3 = 132, 140, 144
    C_gcb1, C_gcb2 = 146, 150
    C_sc3 = 152
    C_gb = 156
    C_wbBC = 164
    C_wbZ = 183
    C_kb = 200
    C_carry = 204
    C_mp = 212
    C_gcv = 276
    C_gf = 292
    C_dtT = 300
    C_tmp = 428
    C_eps = 500
    C_one = 502

    def init_consts():
        gp(lambda e: e.memset(SMALL[:, C_eps:C_eps + 1], EPS))
        gp(lambda e: e.memset(SMALL[:, C_one:C_one + 2], 1.0))
        gp(lambda e: e.memset(ONESC[:], 1.0))
        gp(lambda e: e.memset(MONES[:], -1.0))
        gp(lambda e: e.memset(E2A[0:1, :], 1.0))
        gp(lambda e: e.memset(IDN[:], 1.0))
        em.sfence("gpsimd")
        gp(lambda e: e.affine_select(IDN[:], IDN[:], pattern=[[-1, 128]],
                                     compare_op=OP.is_equal, fill=0.0, base=0,
                                     channel_multiplier=1))
        gp(lambda e: e.memset(TRI2[:], 0.0))
        em.sfence("gpsimd")
        gp(lambda e: e.affine_select(TRI2[0:64, :], TRI2[0:64, :], pattern=[[-1, 64]],
                                     compare_op=OP.is_gt, fill=1.0, base=0,
                                     channel_multiplier=1))
        gp(lambda e: e.affine_select(TRI2[64:128, :], TRI2[64:128, :],
                                     pattern=[[-1, 64]], compare_op=OP.is_gt,
                                     fill=1.0, base=0, channel_multiplier=1))
        gp(lambda e: e.memset(I2[:], 1.0))
        em.sfence("gpsimd")
        gp(lambda e: e.affine_select(I2[0:64, :], I2[0:64, :], pattern=[[-1, 64]],
                                     compare_op=OP.is_equal, fill=0.0, base=0,
                                     channel_multiplier=1))
        gp(lambda e: e.affine_select(I2[64:128, :], I2[64:128, :], pattern=[[-1, 64]],
                                     compare_op=OP.is_equal, fill=0.0, base=0,
                                     channel_multiplier=1))
        gp(lambda e: e.memset(SL2[:], 0.0))
        gp(lambda e: e.memset(SL2[0:64, 0:64], 1.0))
        em.sfence("gpsimd")
        gp(lambda e: e.affine_select(SL2[0:64, 0:64], SL2[0:64, 0:64],
                                     pattern=[[-1, 64]], compare_op=OP.is_gt,
                                     fill=0.0, base=0, channel_multiplier=1))
        gp(lambda e: e.memset(SL2[64:128, 64:128], 1.0))
        em.sfence("gpsimd")
        gp(lambda e: e.affine_select(SL2[64:128, 64:128], SL2[64:128, 64:128],
                                     pattern=[[-1, 64]], compare_op=OP.is_gt,
                                     fill=0.0, base=0, channel_multiplier=1))
        gp(lambda e: e.memset(SELR[0:32, :], 1.0))
        em.sfence("gpsimd")
        gp(lambda e: e.affine_select(SELR[0:32, :], SELR[0:32, :],
                                     pattern=[[-1, 32], [0, 64]],
                                     compare_op=OP.is_equal, fill=0.0, base=0,
                                     channel_multiplier=1))

    def load_params(scr):
        for dst, src in [(SMc(C_convw, 72), p_convw), (SMc(C_convb, 18), p_convb),
                         (SMc(C_ndtb, 1, 0, 32), p_ndtb),
                         (SMc(C_expA, 1, 0, 32), p_alog),
                         (SMc(C_anw, 16), p_anw), (SMc(C_lng, 8), p_lng),
                         (SMc(C_lnb, 8), p_lnb), (SMc(C_bnw, 8), p_bnw),
                         (SMc(C_cnb1, 8), p_cnb1), (SMc(C_cnb2, 4), p_cnb2),
                         (SMc(C_cnb3, 2), p_cnb3), (SMc(C_gcb1, 4), p_gcb1),
                         (SMc(C_gcb2, 2), p_gcb2), (SMc(C_sc3, 3, 0, 1), p_sc3)]:
            dma_in(dst, src)
        em.fence("sync", ["scalar", "vector"])
        act(lambda e: e.activation(SMc(C_expA, 1, 0, 32), SMc(C_expA, 1, 0, 32),
                                   AF.Exp))
        dve(lambda e: e.tensor_tensor(SMc(C_gb, 8), SMc(C_lng, 8), SMc(C_lnb, 8),
                                      op=OP.mult))
        drf = scr["drf"]
        drb = scr["drb"]
        dma_in(drf, p_dskrow)
        em.fence("sync", "vector")
        dve(lambda e: e.tensor_copy(drb, drf))
        em.fence("vector", "tensor")
        for q in range(4):
            pe(lambda e, q=q: e.matmul(
                PS[6][:], ONESC[0:1, 0:1].to_broadcast([1, 128]),
                drb[0:1, q * 512:(q + 1) * 512], start=True, stop=True))
            em.fence("tensor", "vector")
            dve(lambda e, q=q: e.tensor_tensor(
                DSKI[:, q * 512:(q + 1) * 512].rearrange("p (h l) -> p h l", l=CH),
                PS[6][:].rearrange("p (h l) -> p h l", l=CH),
                I2.unsqueeze(1).to_broadcast([128, 8, CH]), op=OP.mult))
            em.fence("vector", "tensor")

    ar = Arena(ARN, 0, ARENA_COLS)
    PERM_ROWS = {}
    for pair in (("Ahi", "Alo"), ("Bhi", "Blo"), ("C1hi", "C1lo"), ("B1hi", "B1lo")):
        p = ar._alloc(TC // 2)
        PERM_ROWS[pair[0]] = ARNB[0:1, 2 * p:2 * p + TC]
        PERM_ROWS[pair[1]] = ARNB[64:65, 2 * p:2 * p + TC]
    dtsb_pos = ar.save()
    dt_sb = ar.f32(TC, parts=32)
    PERM_BASE = ar.save()

    def bcast_rows(ps_tile, hi_row, lo_row, c0, n=SLAB, accumulate=False,
                   neg=False, stop=True, mhalf=None):
        src_col = MONES if neg else ONESC
        M = 128 if mhalf is None else 64
        o = ps_tile[:, 0:n] if mhalf is None else ps_tile[mhalf:mhalf + 64, 0:n]
        kw = {} if mhalf is None else {"tile_position": (0, mhalf)}
        rows_seq = ((hi_row, not accumulate, stop),)
        for row, st_, sp_ in rows_seq:
            bp = row.base_partition()
            lh = src_col[bp:bp + 1, 0:1].to_broadcast([1, M])
            pe(lambda e, row=row, lh=lh, st_=st_, sp_=sp_: e.matmul(
                o, lh, row[0:1, c0:c0 + n], start=st_, stop=sp_,
                skip_group_check=True, **kw))

    def ln_stats_pass(xsrc, rows, inner=False, spill_x=None):
        base = ar.save()
        xslw = [[ar.f32(SLAB) for _ in range(G)] for _ in range(2)]
        xbf = [ar.bf16(SLAB) for _ in range(G)]
        sqb = [ar.f32(SLAB) for _ in range(G)]
        srow = ar.f32(TC, parts=1)
        qrow = ar.f32(TC, parts=1)
        smt = [ar.f32(32) for _ in range(10)]

        def ls_load(s, rdm=None):
            if rdm is not None:
                em.wait("sync", "vector", rdm[0])
                em.wait("sync", "scalar", rdm[1])
                if spill_x is not None:
                    em.wait("sync", "gpdma", rdm[2])
            for g in range(G):
                dma_in(xslw[s % 2][g], xsrc(g, s * SLAB))
            return em.mark("sync")

        ls_ld = {0: ls_load(0)}
        ls_rdm = {}
        for s in range(NSLAB):
            c0 = s * SLAB
            xsl = xslw[s % 2]
            if s + 1 < NSLAB:
                ls_ld[s + 1] = ls_load(s + 1, ls_rdm.get(s - 1))
            em.wait("vector", "sync", ls_ld[s])
            em.wait("scalar", "sync", ls_ld[s])
            if s > 0:
                em.wait("vector", "tensor", ls_rdm[s - 1][3])
                em.wait("scalar", "tensor", ls_rdm[s - 1][3])
            for g in range(G):
                dve(lambda e, g=g, xsl=xsl: e.tensor_copy(xbf[g], xsl[g]))
                act(lambda e, g=g, xsl=xsl: e.activation(sqb[g], xsl[g],
                                                         AF.Square))
            em.fence(["vector", "scalar"], "tensor")
            if s > 0:
                em.wait("tensor", "scalar", ls_rdm[s - 1][1])
            for g in range(G):
                pe(lambda e, g=g: e.matmul(PS[5][0:1, :], ONESC[:, 0:1], xbf[g],
                                           start=(g == 0), stop=(g == G - 1)))
            for g in range(G):
                pe(lambda e, g=g: e.matmul(PS[6][0:1, :],
                                           SMALL[:, C_one:C_one + 1], sqb[g],
                                           start=(g == 0), stop=(g == G - 1)))
            em.fence("tensor", "scalar")
            act(lambda e, c0=c0: e.copy(srow[0:1, c0:c0 + SLAB], PS[5][0:1, :]))
            act(lambda e, c0=c0: e.copy(qrow[0:1, c0:c0 + SLAB], PS[6][0:1, :]))
            if spill_x is not None:
                em.fence("sync", "gpsimd")
                for g in range(G):
                    dma_out(spill_x(g, c0), xsl[g])
            ls_rdm[s] = (em.mark("vector"), em.mark("scalar"),
                         em.mark("gpdma"), em.mark("tensor"))
        em.fence("scalar", "sync")
        dma_in(smt[0], srow)
        dma_in(smt[1], qrow)
        em.fence("sync", ["vector", "scalar"])
        m, t1, var, r1, Af, Bf = smt[2], smt[3], smt[4], smt[5], smt[6], smt[7]
        dve(lambda e: e.tensor_scalar(m, smt[0], 1.0 / D, None, OP.mult))
        em.sfence("vector")
        dve(lambda e: e.tensor_tensor(t1, m, m, op=OP.mult))
        em.sfence("vector")
        dve(lambda e: e.scalar_tensor_tensor(var, smt[1], 1.0 / D, t1, OP.mult,
                                             OP.subtract))
        em.fence("vector", "scalar")
        act(lambda e: e.activation(r1, var, AF.Sqrt, bias=SMALL[0:128, C_eps:C_eps + 1]))
        em.fence("scalar", "vector")
        dve(lambda e: e.reciprocal(r1, r1))
        em.sfence("vector")
        if inner:
            dve(lambda e: e.tensor_tensor(t1, r1, r1, op=OP.mult))
            em.sfence("vector")
            dve(lambda e: e.tensor_tensor(t1, var, t1, op=OP.mult))
            em.fence("vector", "scalar")
            act(lambda e: e.activation(t1, t1, AF.Sqrt, bias=SMALL[0:128, C_eps:C_eps + 1]))
            em.fence("scalar", "vector")
            dve(lambda e: e.reciprocal(t1, t1))
            em.sfence("vector")
            dve(lambda e: e.tensor_tensor(Af, r1, t1, op=OP.mult))
            em.sfence("vector")
        else:
            Af = r1
        dve(lambda e: e.scalar_tensor_tensor(Bf, m, -1.0, Af, OP.mult, OP.mult))
        em.sfence("vector")
        HIA, LOA, HIB, LOB = (ar.bf16(32) for _ in range(4))
        dve(lambda e: e.tensor_copy(HIA, Af))
        em.sfence("vector")
        dve(lambda e: e.tensor_tensor(LOA, Af, HIA, op=OP.subtract))
        dve(lambda e: e.tensor_copy(HIB, Bf))
        em.sfence("vector")
        dve(lambda e: e.tensor_tensor(LOB, Bf, HIB, op=OP.subtract))
        em.fence("vector", "sync")
        dma_in(rows["Ahi"], HIA)
        dma_in(rows["Alo"], LOA)
        dma_in(rows["Bhi"], HIB)
        dma_in(rows["Blo"], LOB)
        if DBG:
            em.fence("sync", "gpsimd")
            dma_out(hd_srow, srow)
            dma_out(hd_qrow, qrow)
            dma_out(hd_arow, rows["Ahi"])
            dma_out(hd_brow, rows["Bhi"])
        if inner:
            C1f, B1f = smt[8], smt[9]
            dve(lambda e: e.tensor_scalar(C1f, r1, 1.0, None, OP.add))
            dve(lambda e: e.scalar_tensor_tensor(B1f, m, -1.0, r1, OP.mult, OP.mult))
            em.sfence("vector")
            HI2, LO2, HI3, LO3 = (ar.bf16(32) for _ in range(4))
            dve(lambda e: e.tensor_copy(HI2, C1f))
            dve(lambda e: e.tensor_copy(HI3, B1f))
            em.sfence("vector")
            dve(lambda e: e.tensor_tensor(LO2, C1f, HI2, op=OP.subtract))
            dve(lambda e: e.tensor_tensor(LO3, B1f, HI3, op=OP.subtract))
            em.fence("vector", "sync")
            dma_in(rows["C1hi"], HI2)
            dma_in(rows["C1lo"], LO2)
            dma_in(rows["B1hi"], HI3)
            dma_in(rows["B1lo"], LO3)
        em.barrier()
        ar.reset(base)

    def ln_apply_emit(xsl, dst_bf, rows, c0, tmp, bA, bB):
        em.fence("scalar", "tensor")
        bcast_rows(PS[5], rows["Ahi"], rows["Alo"], c0)
        bcast_rows(PS[6], rows["Bhi"], rows["Blo"], c0)
        em.fence("tensor", "scalar")
        act(lambda e: e.copy(bA, PS[5][:]))
        act(lambda e: e.copy(bB, PS[6][:]))
        em.fence("scalar", ["vector", "tensor"])
        for g in range(G):
            dve(lambda e, g=g: e.tensor_tensor(tmp[g], xsl[g], bA, op=OP.mult))
        for g in range(G):
            dve(lambda e, g=g: e.tensor_tensor(dst_bf[g], tmp[g], bB, op=OP.add))

    proj_prev = {0: 0, 1: 0}
    def proj_mm(nfb, ndg, W, rhs, evac, mr=None):
        evac_ct = dict(proj_prev)
        for fb in range(nfb):
            pst = PS[fb % 2]
            m = 128 if mr is None else mr(fb)
            em.wait("tensor", "scalar", evac_ct[fb - 2] if fb >= 2
                    else proj_prev[fb])
            for dg in range(ndg):
                pe(lambda e, fb=fb, dg=dg, pst=pst, m=m: e.matmul(
                    pst[0:m, :], W(dg, fb), rhs(dg), start=(dg == 0),
                    stop=(dg == ndg - 1)))
            em.fence("tensor", "scalar")
            evac(fb, pst)
            evac_ct[fb] = em.mark("scalar")
            proj_prev[fb % 2] = evac_ct[fb]

    scr0 = {}
    base0 = ar.save()
    scr0["drf"] = ar.f32(DI, parts=1)
    scr0["drb"] = ar.bf16(DI, parts=1)
    init_consts()
    em.fence("gpsimd", ["vector", "scalar", "tensor"])
    load_params(scr0)
    em.barrier()
    ar.reset(base0)

    rows_a = PERM_ROWS

    def pass_a_stats():
        ln_stats_pass(lambda g, c0: x_fm[g * 128:(g + 1) * 128, c0:c0 + SLAB],
                      rows_a, inner=True,
                      spill_x=lambda g, c0: h_x[g * 128:(g + 1) * 128, c0:c0 + SLAB])

    def pass_a_AB():
        base = ar.save()
        WBC = [ar.bf16(2336) for _ in range(G)]
        wstg = ar.f32(2336)
        xslw = [[ar.f32(SLAB) for _ in range(G)] for _ in range(2)]
        xnw = [[ar.bf16(SLAB) for _ in range(G)] for _ in range(2)]
        tmp = [ar.f32(SLAB) for _ in range(G)]
        bA = ar.f32(SLAB)
        bB = ar.f32(SLAB)
        stg = [ar.bf16(SLAB) for _ in range(3)]
        for g in range(G):
            dma_in(wstg, wt_in[g * 128:(g + 1) * 128, 2048:4384])
            em.fence("sync", ["vector", "tensor"])
            dve(lambda e, g=g: e.tensor_scalar(WBC[g], wstg,
                                               SMc(C_lng, 8)[:, g:g + 1],
                                               None, OP.mult))
            for fb in range(19 if "no_wb" not in HACK else 0):
                mr = 128 if fb < 18 else 32
                pe(lambda e, g=g, fb=fb, mr=mr: e.matmul(
                    PS[2][0:mr, fb:fb + 1], wstg[:, fb * 128:fb * 128 + mr],
                    SMc(C_lnb, 8)[:, g:g + 1], start=(g == 0 and fb == 0),
                    stop=(g == G - 1 and fb == 18), skip_group_check=True))
            em.fence(["vector", "tensor"], "sync")
        em.fence("tensor", "scalar")
        if "no_wb" in HACK:
            gp(lambda e: e.memset(SMc(C_wbBC, 19), 0.0))
        else:
            act(lambda e: e.copy(SMc(C_wbBC, 18), PS[2][:, 0:18]))
            act(lambda e: e.copy(SMALL[0:32, C_wbBC + 18:C_wbBC + 19],
                                 PS[2][0:32, 18:19]))
        em.barrier()

        def ab_load(s, vm=None):
            if vm is not None:
                em.wait("sync", "vector", vm)
            for g in range(G):
                dma_in(xslw[s % 2][g],
                       x_fm[g * 128:(g + 1) * 128, s * SLAB:(s + 1) * SLAB])
            return em.mark("sync")

        ab_ld = {0: ab_load(0)}
        ab_rdm = {}
        for s in range(NSLAB):
            c0 = s * SLAB
            xsl = xslw[s % 2]
            xn = xnw[s % 2]
            if s + 1 < NSLAB:
                ab_ld[s + 1] = ab_load(s + 1,
                                       ab_rdm[s - 1][0] if s > 0 else None)
            em.wait("vector", "sync", ab_ld[s])
            if s >= 2:
                em.wait("vector", "tensor", ab_rdm[s - 2][1])
                em.wait("vector", "gpdma", ab_rdm[s - 2][2])
            if s >= 1:
                em.wait("scalar", "vector", ab_rdm[s - 1][0])
            ln_apply_emit(xsl, xn, rows_a, c0, tmp, bA, bB)
            em.fence("vector", ["gpsimd", "tensor"])
            for g in range(G):
                dma_out(h_xn[g * 128:(g + 1) * 128, c0:c0 + SLAB], xn[g])

            def evac(fb, pst, c0=c0):
                if fb < 18:
                    st_t = stg[fb % 3]
                    em.wait("scalar", "gpdma")
                    act(lambda e, fb=fb, pst=pst, st_t=st_t: e.activation(
                        st_t, pst[:], AF.Identity,
                        bias=SMc(C_wbBC, 19)[:, fb:fb + 1]))
                    em.fence("scalar", "gpsimd")
                    dma_out(h_xbc[fb * 128:(fb + 1) * 128, c0:c0 + SLAB], st_t)
                else:
                    act(lambda e, pst=pst: e.activation(
                        dt_sb[:, c0:c0 + SLAB], pst[0:32, :], AF.Identity,
                        bias=SMc(C_wbBC, 19)[0:32, 18:19]))
            proj_mm(19, G,
                    lambda dg, fb: WBC[dg][:, fb * 128:fb * 128 +
                                           (128 if fb < 18 else 32)],
                    lambda dg, xn=xn: xn[dg], evac,
                    mr=lambda fb: 128 if fb < 18 else 32)
            ab_rdm[s] = (em.mark("vector"), em.mark("tensor"),
                         em.mark("gpdma"))
        em.barrier()
        ar.reset(base)

    def pass_a_Z():
        base = ar.save()
        WZ = [ar.bf16(2048) for _ in range(G)]
        wstg = ar.f32(2048)
        xnb = [[ar.bf16(SLAB) for _ in range(G)] for _ in range(2)]
        zst = [ar.f32(SLAB) for _ in range(3)]
        for g in range(G):
            dma_in(wstg, wt_in[g * 128:(g + 1) * 128, 0:2048])
            em.fence("sync", ["vector", "tensor"])
            dve(lambda e, g=g: e.tensor_scalar(WZ[g], wstg,
                                               SMc(C_lng, 8)[:, g:g + 1],
                                               None, OP.mult))
            for fb in range(16 if "no_wb" not in HACK else 0):
                pe(lambda e, g=g, fb=fb: e.matmul(
                    PS[2][:, 19 + fb:20 + fb], wstg[:, fb * 128:(fb + 1) * 128],
                    SMc(C_lnb, 8)[:, g:g + 1], start=(g == 0 and fb == 0),
                    stop=(g == G - 1 and fb == 15), skip_group_check=True))
            em.fence(["vector", "tensor"], "sync")
        em.fence("tensor", "scalar")
        if "no_wb" in HACK:
            gp(lambda e: e.memset(SMc(C_wbZ, 16), 0.0))
        else:
            act(lambda e: e.copy(SMc(C_wbZ, 16), PS[2][:, 19:35]))
        em.barrier()

        def _z_slab(s, c0, cur):
            def evac(fb, pst, c0=c0):
                zt = zst[fb % 3]
                em.wait("scalar", "gpdma")
                act(lambda e, pst=pst, zt=zt, fb=fb: e.activation(
                    zt, pst[:], AF.Identity, bias=SMc(C_wbZ, 16)[:, fb:fb + 1]))
                em.fence("scalar", "gpsimd")
                dma_out(h_z[fb * 128:(fb + 1) * 128, c0:c0 + SLAB], zt)
            proj_mm(16, G, lambda dg, fb: WZ[dg][:, fb * 128:(fb + 1) * 128],
                    lambda dg, cur=cur: cur[dg], evac)
        if True:
            for s in range(NSLAB):
                c0 = s * SLAB
                cur = xnb[0]
                em.fence("tensor", "sync")
                for g in range(G):
                    dma_in(cur[g], h_xn[g * 128:(g + 1) * 128, c0:c0 + SLAB])
                em.fence("sync", "tensor")
                _z_slab(s, c0, cur)
            em.barrier()
            ar.reset(base)
            return
        for g in range(G):
            dma_in(xnb[0][g], h_xn[g * 128:(g + 1) * 128, 0:SLAB])
        for s in range(NSLAB):
            c0 = s * SLAB
            cur = xnb[s % 2]
            if s + 1 < NSLAB:
                if s >= 1:
                    em.wait("sync", "tensor")
                for g in range(G):
                    dma_in(xnb[(s + 1) % 2][g],
                           h_xn[g * 128:(g + 1) * 128, c0 + SLAB:c0 + 2 * SLAB])
            em.fence("sync", "tensor")

            _z_slab(s, c0, cur)
        em.barrier()
        ar.reset(base)

    def pass_a_C1():
        base = ar.save()
        diag = [ar.bf16(128) for _ in range(72)]
        blk = ar._alloc(GC * 260 + GC * 256)
        xbcin = [ARNB[:, 2 * (blk + g * 260):2 * (blk + g * 260) + 520]
                 for g in range(GC)]
        xsb = blk + GC * 260
        xs = [ARNB[:, 2 * (xsb + g * 256):2 * (xsb + g * 256) + SLAB]
              for g in range(GC)]
        YM = ARN[:, blk:blk + GI * SLAB]
        assert xsb + 16 * 256 >= blk + GI * SLAB, "Bm/Cm overlap YM alias"
        xh_tm = [ar.bf16(DI) for _ in range(TBS)]
        xhd = [ar.bf16(DI) for _ in range(TBS)]
        B_tm = [ar.bf16(128) for _ in range(TBS)]
        aTri = ar.bf16(DI)
        Etile = ar.bf16(DI)
        GmS = ar.bf16(64)
        S = ar.f32(DI)
        Sbf = ar.bf16(DI)
        dtw = [ar.f32(SLAB, parts=32) for _ in range(6)]
        abf = ar.bf16(SLAB, parts=32)
        eac = ar.bf16(SLAB, parts=32)
        dtbf = ar.bf16(SLAB, parts=32)
        wstbf = ar.bf16(SLAB, parts=32)
        earow = [ar.bf16(DI, parts=1) for _ in range(2)]
        a_tm = [ar.bf16(32) for _ in range(TBS)]
        wst_tm = [ar.bf16(32) for _ in range(TBS)]
        gsm = ar.f32(CPS, parts=32)
        ghi = ar.bf16(CPS, parts=32)
        glo = ar.bf16(CPS, parts=32)
        ztile = [ar.f32(SLAB) for _ in range(2)]
        tEA = aTri[:, 0:SLAB]
        sqb = [ar.bf16(SLAB) for _ in range(2)]
        ynst = [ar.bf16(SLAB) for _ in range(2)]
        bR = ar.f32(SLAB)
        rrow = ar.f32(SLAB, parts=1)
        rhi = ar.bf16(SLAB, parts=1)
        rlo = ar.bf16(SLAB, parts=1)

        def ym(g):
            return YM[:, g * SLAB:(g + 1) * SLAB]

        em.fence("gpsimd", "vector")
        for g in range(GC):
            for k in range(4):
                dve(lambda e, g=g, k=k: e.tensor_scalar(
                    diag[g * 4 + k], IDN,
                    SMc(C_convw, 72)[:, g * 4 + k:g * 4 + k + 1], None, OP.mult))
        em.barrier()

        for s in range(NSLAB):
            c0 = s * SLAB
            seq_start = (s % SPB == 0)
            for g in range(GC):
                if seq_start:
                    gp(lambda e, g=g: e.memset(xbcin[g][:, 0:3], 0.0))
                    dma_in(xbcin[g][:, 3:515],
                           h_xbc[g * 128:(g + 1) * 128, c0:c0 + SLAB])
                else:
                    dma_in(xbcin[g][:, 0:515],
                           h_xbc[g * 128:(g + 1) * 128, c0 - 3:c0 + SLAB])
            em.fence(["sync", "gpsimd"], "tensor")
            ea_dma = {}
            ea_rd = {}

            def dt_seg_a():
                dtr = dt_sb[:, c0:c0 + SLAB]
                act(lambda e, dtr=dtr: e.activation(dtw[0][:], dtr, AF.Sigmoid,
                                                    bias=SMc(C_ndtb, 1, 0, 32),
                                                    scale=-1.0))
                em.sfence("scalar")
                act(lambda e: e.activation(dtw[0][:], dtw[0][:], AF.Ln))
                em.fence("scalar", "vector")
                dve(lambda e: e.tensor_scalar(dtw[1][:], dtw[0][:], -1.0, None,
                                              OP.mult))
                dve(lambda e: e.tensor_scalar(dtw[2][:], dtw[0][:],
                                              SMc(C_expA, 1, 0, 32), None,
                                              OP.mult))
                em.sfence("vector")
                dve(lambda e: e.tensor_copy(abf, dtw[2][:]))
                dve(lambda e: e.tensor_copy(dtbf, dtw[1][:]))
                for ci in range(CPS):
                    dve(lambda e, ci=ci: e.tensor_tensor_scan(
                        dtw[3][:, ci * CH:(ci + 1) * CH], ONESC[0:32, 0:CH],
                        dtw[2][:, ci * CH:(ci + 1) * CH], 0.0, OP.mult, OP.add))

            def dt_seg_b():
                em.fence("vector", "scalar")
                act(lambda e: e.activation(gsm[:], dtw[3][:, CH - 1::CH], AF.Exp))
                act(lambda e: e.activation(eac[:], dtw[3][:], AF.Exp))
                em.fence("scalar", "sync")
                dma_in(earow[0][0:1, :], eac[:, 0:CH])
                ea_dma[0] = em.mark("sync")
                em.fence("scalar", "vector")
                em.sfence("vector")
                dve(lambda e: e.tensor_tensor(
                    dtw[4][:].rearrange("p (c l) -> p c l", l=CH),
                    dtw[3][:, CH - 1::CH].unsqueeze(2).to_broadcast(
                        [32, CPS, CH]),
                    dtw[3][:].rearrange("p (c l) -> p c l", l=CH),
                    op=OP.subtract))

            def dt_seg_c():
                em.fence("vector", "scalar")
                act(lambda e: e.activation(dtw[4][:], dtw[4][:], AF.Exp))
                em.fence("scalar", "vector")
                dve(lambda e: e.tensor_tensor(dtw[5][:], dtw[1][:], dtw[4][:],
                                              op=OP.mult))
                dve(lambda e: e.tensor_copy(ghi[:], gsm[:]))
                em.sfence("vector")
                dve(lambda e: e.tensor_copy(wstbf, dtw[5][:]))
                dve(lambda e: e.tensor_tensor(glo[:], gsm[:], ghi[:],
                                              op=OP.subtract))

            dtsegs = {0: dt_seg_a, 6: dt_seg_b, 11: dt_seg_c}
            cbanks = (PS[0], PS[1], PS[5], PS[6])
            evac_ct = {}
            for g in range(GC):
                if g in dtsegs:
                    dtsegs[g]()
                pst = cbanks[g % 4]
                if g >= 4:
                    em.wait("tensor", "scalar", evac_ct[g - 4])
                for k in range(4):
                    pe(lambda e, g=g, k=k, pst=pst: e.matmul(
                        pst[:], diag[g * 4 + k], xbcin[g][:, k:k + SLAB],
                        start=(k == 0), stop=(k == 3)))
                em.fence("tensor", "scalar")
                act(lambda e, g=g, pst=pst: e.activation(
                    xs[g], pst[:], AF.Silu, bias=SMc(C_convb, 18)[:, g:g + 1]))
                evac_ct[g] = em.mark("scalar")
                if DBG:
                    em.fence("scalar", "gpsimd")
                    dma_out(hd_xs[g * 128:(g + 1) * 128, c0:c0 + SLAB], xs[g])
            if DBG:
                em.fence("vector", "gpsimd")
                em.fence("vector", "sync")
                em.fence("scalar", "sync")
                dma_out_sync(hd_dtr[:, c0:c0 + SLAB], dt_sb[:, c0:c0 + SLAB])
                dma_out_sync(hd_sp[:, c0:c0 + SLAB], dtw[0][:])
                dma_out_sync(hd_dt[:, c0:c0 + SLAB], dtw[1][:])
                dma_out_sync(hd_acs[:, c0:c0 + SLAB], dtw[3][:])
                dma_out_sync(hd_wst[:, c0:c0 + SLAB], dtw[5][:])
            em.fence("vector", "tensor")
            em.fence("scalar", "tensor")
            tbanks = (PT, PTB2, PTB3)
            tevac = {0: 0, 1: 0, 2: 0}
            for tb in range(TBS):
                for grp in range(3):
                    bk = (tb * 3 + grp) % 3
                    bank = tbanks[bk]
                    em.wait("tensor", "scalar", tevac[bk])
                    if grp == 0:
                        pe(lambda e, tb=tb, bank=bank: e.transpose(
                            bank[:, 0:32], abf[:, tb * 128:(tb + 1) * 128],
                            IDN[0:32, 0:32]))
                        pe(lambda e, tb=tb, bank=bank: e.transpose(
                            bank[:, 32:64], wstbf[:, tb * 128:(tb + 1) * 128],
                            IDN[0:32, 0:32]))
                        pe(lambda e, tb=tb, bank=bank: e.transpose(
                            bank[:, 64:192], xs[16][:, tb * 128:(tb + 1) * 128],
                            IDN))
                        for ci2 in range(2):
                            ci = tb * 2 + ci2
                            half = ci2 * 64
                            pe(lambda e, ci=ci, half=half, bank=bank: e.transpose(
                                bank[half:half + 64, 192:224],
                                dtbf[:, ci * CH:(ci + 1) * CH], IDN[0:32, 0:32],
                                tile_position=(0, half)))
                        mk = em.mark("tensor")
                        em.wait("scalar", "tensor", mk)
                        act(lambda e, tb=tb, bank=bank: e.copy(a_tm[tb],
                                                               bank[:, 0:32]))
                        act(lambda e, tb=tb, bank=bank: e.copy(wst_tm[tb],
                                                               bank[:, 32:64]))
                        act(lambda e, tb=tb, bank=bank: e.copy(B_tm[tb],
                                                               bank[:, 64:192]))
                        for half in (0, 64):
                            act(lambda e, tb=tb, half=half, bank=bank: e.copy(
                                SMc(C_dtT, 128)[half:half + 64,
                                                tb * 32:(tb + 1) * 32],
                                bank[half:half + 64, 192:224]))
                    else:
                        i8 = grp - 1
                        for gg in range(8):
                            pe(lambda e, tb=tb, i8=i8, gg=gg, bank=bank:
                               e.transpose(
                                   bank[:, gg * 128:(gg + 1) * 128],
                                   xs[i8 * 8 + gg][:, tb * 128:(tb + 1) * 128],
                                   IDN))
                        mk = em.mark("tensor")
                        em.wait("scalar", "tensor", mk)
                        for hf in range(2):
                            act(lambda e, tb=tb, i8=i8, hf=hf, bank=bank: e.copy(
                                xh_tm[tb][:, i8 * 1024 + hf * 512:
                                          i8 * 1024 + (hf + 1) * 512],
                                bank[:, hf * 512:(hf + 1) * 512]))
                    tevac[bk] = em.mark("scalar")
                em.wait("vector", "scalar", tevac[(tb * 3 + 2) % 3])
                dve(lambda e, tb=tb: e.tensor_tensor(
                    xhd[tb].rearrange("p (h q) -> p h q", q=HD),
                    xh_tm[tb].rearrange("p (h q) -> p h q", q=HD),
                    wst_tm[tb].unsqueeze(2).to_broadcast([128, 32, HD]),
                    op=OP.mult))
            em.fence("vector", "tensor")
            em.fence("scalar", "tensor")
            for gz in range(2):
                dma_in(ztile[gz], h_z[gz * 128:(gz + 1) * 128, c0:c0 + SLAB])
            if seq_start:
                dve(lambda e: e.memset(S[:], 0.0))
                dve(lambda e: e.memset(Sbf[:], 0.0))
                em.fence("vector", "tensor")
            em.fence("scalar", "gpsimd")
            for cp in range(TBS):
                dve(lambda e, cp=cp: e.tensor_tensor(
                    aTri.rearrange("p (h l) -> p h l", l=CH)[:, 0:HSP, :],
                    a_tm[cp].unsqueeze(2).to_broadcast([128, 32, CH])[:, 0:HSP, :],
                    TRI2.unsqueeze(1).to_broadcast([128, 32, CH])[:, 0:HSP, :],
                    op=OP.mult))
                gp(lambda e, cp=cp: e.tensor_tensor(
                    aTri.rearrange("p (h l) -> p h l", l=CH)[:, HSP:32, :],
                    a_tm[cp].unsqueeze(2).to_broadcast([128, 32, CH])[:, HSP:32, :],
                    TRI2.unsqueeze(1).to_broadcast([128, 32, CH])[:, HSP:32, :],
                    op=OP.mult))
                em.fence(["vector", "gpsimd"], "tensor")
                for q in range(4):
                    pq = PS[q % 2]
                    if q >= 2:
                        em.wait("tensor", "scalar")
                    pe(lambda e, q=q, pq=pq: e.matmul(
                        pq[:], SL2, aTri[:, q * 512:(q + 1) * 512],
                        start=True, stop=True))
                    em.fence("tensor", "scalar")
                    act(lambda e, q=q, pq=pq: e.activation(
                        Etile[:, q * 512:(q + 1) * 512], pq[:], AF.Exp))
                em.fence("scalar", ["tensor", "vector", "gpsimd"])
                for ci2 in range(2):
                    ci = cp * 2 + ci2
                    half = ci2 * 64
                    pe(lambda e, ci=ci, half=half: e.matmul(
                        PS[2][half:half + 64, 0:64],
                        xs[16][:, ci * CH:(ci + 1) * CH],
                        xs[17][:, ci * CH:(ci + 1) * CH],
                        start=True, stop=True, tile_position=(0, half),
                        skip_group_check=True))
                    em.fence("tensor", "vector")
                    dve(lambda e, half=half: e.tensor_tensor(
                        GmS[half:half + 64, :], PS[2][half:half + 64, 0:64],
                        TRI2[half:half + 64, :], op=OP.mult))
                em.sfence("vector")
                em.fence("vector", "gpsimd")
                dve(lambda e, cp=cp: e.tensor_tensor(
                    Etile.rearrange("p (h l) -> p h l", l=CH)[:, 0:HSP, :],
                    Etile.rearrange("p (h l) -> p h l", l=CH)[:, 0:HSP, :],
                    GmS.unsqueeze(1).to_broadcast([128, 32, CH])[:, 0:HSP, :],
                    op=OP.mult))
                gp(lambda e, cp=cp: e.tensor_tensor(
                    Etile.rearrange("p (h l) -> p h l", l=CH)[:, HSP:32, :],
                    Etile.rearrange("p (h l) -> p h l", l=CH)[:, HSP:32, :],
                    GmS.unsqueeze(1).to_broadcast([128, 32, CH])[:, HSP:32, :],
                    op=OP.mult))
                dve(lambda e, cp=cp: e.tensor_tensor(
                    Etile.rearrange("p (h l) -> p h l", l=CH)[:, 0:HSP, :],
                    Etile.rearrange("p (h l) -> p h l", l=CH)[:, 0:HSP, :],
                    SMc(C_dtT, 128)[:, cp * 32:(cp + 1) * 32].unsqueeze(2)
                    .to_broadcast([128, 32, CH])[:, 0:HSP, :], op=OP.mult))
                gp(lambda e, cp=cp: e.tensor_tensor(
                    Etile.rearrange("p (h l) -> p h l", l=CH)[:, HSP:32, :],
                    Etile.rearrange("p (h l) -> p h l", l=CH)[:, HSP:32, :],
                    SMc(C_dtT, 128)[:, cp * 32:(cp + 1) * 32].unsqueeze(2)
                    .to_broadcast([128, 32, CH])[:, HSP:32, :], op=OP.mult))
                dve(lambda e: e.tensor_tensor(Etile[:, 0:HSP * CH],
                                              Etile[:, 0:HSP * CH],
                                              DSKI[:, 0:HSP * CH], op=OP.add))
                gp(lambda e: e.tensor_tensor(Etile[:, HSP * CH:DI],
                                             Etile[:, HSP * CH:DI],
                                             DSKI[:, HSP * CH:DI], op=OP.add))
                if "pb1" in HACK:
                    em.barrier()
                if DBG:
                    em.fence("vector", "sync")
                    if True:
                        dma_out_sync(hd_mp[:, cp * DI:(cp + 1) * DI], Etile[:])
                        dma_out_sync(hd_xhtm[:, cp * DI:(cp + 1) * DI], xh_tm[cp][:])
                    em.fence("sync", "vector")
                em.fence(["vector", "gpsimd"], "tensor")
                for ci2 in range(2):
                    ci = cp * 2 + ci2
                    half = ci2 * 64
                    gci = s % SPB * CPS + ci
                    gc_first = (gci == 0)
                    if ci + 1 < 2 * TBS:
                        if ci - 1 in ea_rd:
                            em.wait("sync", "tensor", ea_rd[ci - 1])
                        dma_in(earow[(ci + 1) % 2][0:1, :],
                               eac[:, (ci + 1) * CH:(ci + 2) * CH])
                        ea_dma[ci + 1] = em.mark("sync")
                    em.wait("tensor", "sync", ea_dma[ci])
                    for h in range(NH):
                        tp = (half, (h % 2) * 64)
                        pst = PS[5] if h < 16 else PS[6]
                        colr = ((h // 2) % 8) * 64
                        pe(lambda e, h=h, half=half, cp=cp, pst=pst, colr=colr,
                           tp=tp: e.matmul(
                               pst[tp[1]:tp[1] + 64, colr:colr + 64],
                               xh_tm[cp][half:half + 64, h * 64:(h + 1) * 64],
                               Etile[half:half + 64, h * 64:(h + 1) * 64],
                               start=(h in (0, 1, 16, 17)),
                               stop=(h in (14, 15, 30, 31)),
                               tile_position=tp, skip_group_check=True))
                    if not gc_first:
                        for h in range(NH):
                            tp = (0, (h % 2) * 64)
                            pst = PS[0] if h < 16 else PS[1]
                            colr = ((h // 2) % 8) * 64
                            pe(lambda e, h=h, ci=ci, pst=pst, colr=colr, tp=tp:
                               e.matmul(pst[tp[1]:tp[1] + 64, colr:colr + 64],
                                        Sbf[:, h * 64:(h + 1) * 64],
                                        xs[17][:, ci * CH:(ci + 1) * CH],
                                        start=(h in (0, 1, 16, 17)),
                                        stop=(h in (14, 15, 30, 31)),
                                        tile_position=tp, skip_group_check=True))
                        for ip, pst in enumerate((PS[2], PS[4])):
                            for hh in range(2):
                                rh = earow[ci % 2].rearrange("o (h l) -> o h l",
                                                             l=CH)
                                pe(lambda e, ip=ip, hh=hh, pst=pst, rh=rh:
                                   e.matmul(
                                       pst[hh * 64:hh * 64 + 64, :],
                                       ONESC[0:1, 0:1].to_broadcast([1, 64]),
                                       rh[0:1, ip * 16 + hh:(ip + 1) * 16:2, :],
                                       start=True, stop=True,
                                       tile_position=(0, hh * 64),
                                       skip_group_check=True))
                    ea_rd[ci] = em.mark("tensor")
                    if "pb2" in HACK:
                        em.barrier()
                    em.fence("tensor", ["scalar", "vector"])
                    for ip, (pd, po, pea) in enumerate(
                            ((PS[5], PS[0], PS[2]), (PS[6], PS[1], PS[4]))):
                        act(lambda e, pd=pd, ip=ip, ci=ci: e.copy(
                            YM.rearrange("p (g t) -> p g t", t=SLAB)
                              [:, ip * 8:(ip + 1) * 8, ci * CH:(ci + 1) * CH],
                            pd[:].rearrange("p (g l) -> p g l", l=CH)))
                        if not gc_first:
                            act(lambda e, pea=pea, ip=ip: e.copy(ynst[ip], pea[:]))
                            em.fence("scalar", "vector")
                            dve(lambda e, po=po, ip=ip: e.tensor_tensor(
                                tEA, po[:], ynst[ip], op=OP.mult))
                            em.fence("vector", "vector") if False else None
                            dve(lambda e, ip=ip, ci=ci: e.tensor_tensor(
                                YM.rearrange("p (g t) -> p g t", t=SLAB)
                                  [:, ip * 8:(ip + 1) * 8, ci * CH:(ci + 1) * CH],
                                YM.rearrange("p (g t) -> p g t", t=SLAB)
                                  [:, ip * 8:(ip + 1) * 8, ci * CH:(ci + 1) * CH],
                                tEA.rearrange("p (g l) -> p g l", l=CH),
                                op=OP.add))
                    if "pb3" in HACK:
                        em.barrier()
                    em.fence(["scalar", "vector"], "tensor")
                    mm_mark = [0] * 4
                    dve_mark = [0] * 4
                    for q in range(4):
                        pb, pg = (PS[3], PS[4]) if q % 2 == 0 else (PS[5], PS[6])
                        if q >= 2:
                            em.wait("tensor", "vector", dve_mark[q - 2])
                        pe(lambda e, half=half, cp=cp, q=q, pb=pb: e.matmul(
                            pb[:], B_tm[cp][half:half + 64, :],
                            xhd[cp][half:half + 64, q * 512:(q + 1) * 512],
                            start=True, stop=True))
                        pe(lambda e, ci=ci, q=q, pg=pg: e.matmul(
                            pg[:], ghi[:, ci:ci + 1].to_broadcast([32, 128]),
                            SELR[0:32, q * 512:(q + 1) * 512],
                            start=True, stop=False, skip_group_check=True))
                        pe(lambda e, ci=ci, q=q, pg=pg: e.matmul(
                            pg[:], glo[:, ci:ci + 1].to_broadcast([32, 128]),
                            SELR[0:32, q * 512:(q + 1) * 512],
                            start=False, stop=True, skip_group_check=True))
                        mm_mark[q] = em.mark("tensor")
                        em.wait("vector", "tensor", mm_mark[q])
                        dve(lambda e, q=q, pg=pg: e.tensor_tensor(
                            S[:, q * 512:(q + 1) * 512],
                            S[:, q * 512:(q + 1) * 512], pg[:], op=OP.mult))
                        dve(lambda e, q=q, pb=pb: e.tensor_tensor(
                            S[:, q * 512:(q + 1) * 512],
                            S[:, q * 512:(q + 1) * 512], pb[:], op=OP.add))
                        dve_mark[q] = em.mark("vector")
                        em.wait("scalar", "vector", dve_mark[q])
                        act(lambda e, q=q: e.copy(Sbf[:, q * 512:(q + 1) * 512],
                                                  S[:, q * 512:(q + 1) * 512]))
                    if "pb4" in HACK:
                        em.barrier()
                    em.fence("scalar", "tensor")
                    if "pb5" in HACK:
                        em.barrier()
            if DBG:
                em.fence("scalar", "gpsimd")
                em.fence("vector", "gpsimd")
                for gdd in range(GI):
                    dma_out(hd_ym[gdd * 128:(gdd + 1) * 128, c0:c0 + SLAB],
                            YM[:, gdd * SLAB:(gdd + 1) * SLAB])
                dma_out(hd_s[:, s * DI:(s + 1) * DI], S[:])
                em.fence("gpdma", "vector")
            for g in range(GI):
                zt = ztile[g % 2]
                em.fence("sync", "scalar")
                act(lambda e, zt=zt: e.activation(zt, zt, AF.Silu))
                em.fence("scalar", "vector")
                dve(lambda e, g=g, zt=zt: e.tensor_tensor(ym(g), ym(g), zt,
                                                          op=OP.mult))
                if g + 1 < GI:
                    em.fence("vector", "sync")
                    dma_in(ztile[(g + 1) % 2],
                           h_z[(g + 1) * 128:(g + 2) * 128, c0:c0 + SLAB])
            em.fence("vector", "scalar")
            for g in range(GI):
                if g >= 2:
                    em.wait("scalar", "tensor")
                act(lambda e, g=g: e.activation(sqb[g % 2], ym(g), AF.Square))
                em.fence("scalar", "tensor")
                pe(lambda e, g=g: e.matmul(PS[2][0:1, :], ONESC[:, 2:3],
                                           sqb[g % 2], start=(g == 0),
                                           stop=(g == GI - 1)))
            em.fence("tensor", "scalar")
            act(lambda e: e.copy(rrow, PS[2][0:1, :]))
            em.fence("scalar", "vector")
            dve(lambda e: e.tensor_scalar(rrow, rrow, 1.0 / DI, None, OP.mult))
            em.fence("vector", "scalar")
            act(lambda e: e.activation(rrow, rrow, AF.Sqrt, bias=SMALL[0:1, C_eps:C_eps + 1]))
            em.fence("scalar", "vector")
            dve(lambda e: e.reciprocal(rrow, rrow))
            em.sfence("vector")
            dve(lambda e: e.tensor_copy(rhi, rrow))
            em.sfence("vector")
            dve(lambda e: e.tensor_tensor(rlo, rrow, rhi, op=OP.subtract))
            em.fence("vector", "tensor")
            bcast_rows(PS[4], rhi, rlo, 0, n=SLAB)
            em.fence("tensor", "scalar")
            act(lambda e: e.copy(bR, PS[4][:]))
            em.fence("scalar", "vector")
            for g in range(GI):
                yt = ynst[g % 2]
                if g >= 2:
                    em.wait("vector", "gpdma")
                dve(lambda e, g=g, yt=yt: e.scalar_tensor_tensor(
                    yt, ym(g), SMc(C_anw, 16)[:, g:g + 1], bR, OP.mult, OP.mult))
                em.fence("vector", "gpsimd")
                dma_out(h_yn[g * 128:(g + 1) * 128, c0:c0 + SLAB], yt)
            em.barrier()
        ar.reset(base)

    def pass_a_C2():
        base = ar.save()
        WO = [ar.bf16(1024) for _ in range(GI)]
        wstg = ar.f32(1024)
        yn = [ar.bf16(SLAB) for _ in range(GI)]
        xsl = [ar.f32(SLAB) for _ in range(G)]
        bC = ar.f32(SLAB)
        for kg in range(GI):
            dma_in(wstg, wt_out[kg * 128:(kg + 1) * 128, :])
            em.fence("sync", "vector")
            dve(lambda e, kg=kg: e.tensor_copy(WO[kg], wstg))
            em.fence("vector", "sync")
        em.barrier()
        for s in range(NSLAB):
            c0 = s * SLAB
            for kg in range(GI):
                dma_in(yn[kg], h_yn[kg * 128:(kg + 1) * 128, c0:c0 + SLAB])
            for g in range(G):
                dma_in(xsl[g], h_x[g * 128:(g + 1) * 128, c0:c0 + SLAB])
            bcast_rows(PS[4], rows_a["C1hi"], rows_a["C1lo"], c0)
            em.fence("tensor", "scalar")
            act(lambda e: e.copy(bC, PS[4][:]))
            em.fence("scalar", "vector")
            em.fence("sync", ["tensor", "vector"])
            for dg in range(G):
                pst = PS[dg % 2]
                if dg >= 2:
                    em.wait("tensor", "vector")
                for kg in range(GI):
                    pe(lambda e, dg=dg, kg=kg, pst=pst: e.matmul(
                        pst[:], WO[kg][:, dg * 128:(dg + 1) * 128], yn[kg],
                        start=(kg == 0), stop=False, skip_group_check=True))
                bcast_rows(pst, rows_a["B1hi"], rows_a["B1lo"], c0,
                           accumulate=True)
                em.fence("tensor", "vector")
                dve(lambda e, dg=dg: e.tensor_tensor(xsl[dg], xsl[dg], bC,
                                                     op=OP.mult))
                em.sfence("vector")
                dve(lambda e, dg=dg, pst=pst: e.tensor_tensor(
                    xsl[dg], xsl[dg], pst[:], op=OP.add))
                em.fence("vector", "gpsimd")
                dma_out(h_x[dg * 128:(dg + 1) * 128, c0:c0 + SLAB], xsl[dg])
            em.barrier()
        ar.reset(base)

    def pass_ffn(last=False):
        dst_hbm = out_fm if last else h_x
        ln_stats_pass(lambda g, c0: h_x[g * 128:(g + 1) * 128, c0:c0 + SLAB],
                      rows_a)
        base = ar.save()
        xsl2 = [[ar.f32(SLAB) for _ in range(G)] for _ in range(2)]
        xln2 = [[ar.bf16(SLAB) for _ in range(G)] for _ in range(2)]
        tmp = [ar.f32(SLAB) for _ in range(G)]
        bA2 = [ar.f32(SLAB) for _ in range(2)]
        bB2 = [ar.f32(SLAB) for _ in range(2)]

        def fb_load(s, vm=None):
            if vm is not None:
                em.wait("sync", "vector", vm)
            cl = s * SLAB
            for g in range(G):
                dma_in(xsl2[s % 2][g],
                       h_x[g * 128:(g + 1) * 128, cl:cl + SLAB])
            return em.mark("sync")

        fb_ld = {0: fb_load(0)}
        fb_rdm = {}
        for s in range(NSLAB):
            c0 = s * SLAB
            if s + 1 < NSLAB:
                fb_ld[s + 1] = fb_load(s + 1,
                                       fb_rdm[s - 1][0] if s > 0 else None)
            em.wait("vector", "sync", fb_ld[s])
            if s >= 2:
                em.wait("vector", "gpdma", fb_rdm[s - 2][1])
                em.wait("scalar", "vector", fb_rdm[s - 2][0])
            ln_apply_emit(xsl2[s % 2], xln2[s % 2], rows_a, c0, tmp,
                          bA2[s % 2], bB2[s % 2])
            for g in range(G):
                dve(lambda e, g=g, s=s: e.reduce_sum(
                    SMc(C_mp, 64)[:, s * 8 + g:s * 8 + g + 1], xln2[s % 2][g],
                    axis=mybir.AxisListType.X))
                em.fence("vector", "gpsimd")
                dma_out(h_xlf[g * 128:(g + 1) * 128, c0:c0 + SLAB],
                        xln2[s % 2][g])
            fb_rdm[s] = (em.mark("vector"), em.mark("gpdma"))
        em.barrier()
        ar.reset(base)
        base = ar.save()
        gw1 = [ar.bf16(512) for _ in range(G)]
        gw2 = [ar.bf16(256) for _ in range(4)]
        gw3 = [ar.bf16(1) for _ in range(2)]
        wstg_c = ar.f32(512)
        gcb = [ar.bf16(2) for _ in range(G)]
        h1g = [ar.bf16(2) for _ in range(4)]
        h2g = [ar.bf16(2) for _ in range(2)]
        for g in range(G):
            dma_in(wstg_c, gc_w1[g * 128:(g + 1) * 128, :])
            em.fence("sync", "vector")
            dve(lambda e, g=g: e.tensor_copy(gw1[g], wstg_c))
            em.fence("vector", "sync")
        for k in range(4):
            dma_in(wstg_c[:, 0:256], gc_w2[k * 128:(k + 1) * 128, :])
            em.fence("sync", "vector")
            dve(lambda e, k=k: e.tensor_copy(gw2[k], wstg_c[:, 0:256]))
            em.fence("vector", "sync")
        for k in range(2):
            dma_in(wstg_c[:, 0:1], gc_w3[k * 128:(k + 1) * 128, :])
            em.fence("sync", "vector")
            dve(lambda e, k=k: e.tensor_copy(gw3[k], wstg_c[:, 0:1]))
            em.fence("vector", "sync")
        for g in range(G):
            for b in range(NB):
                dve(lambda e, g=g, b=b: e.reduce_sum(
                    SMc(C_gcv, 16)[:, g * 2 + b:g * 2 + b + 1],
                    SMALL[:, C_mp + b * 32 + g:C_mp + b * 32 + g + 25:8],
                    axis=mybir.AxisListType.X))
        em.sfence("vector")
        dve(lambda e: e.tensor_scalar(SMc(C_gcv, 16), SMc(C_gcv, 16), 1.0 / T,
                                      None, OP.mult))
        em.sfence("vector")
        for g in range(G):
            dve(lambda e, g=g: e.tensor_copy(gcb[g],
                                             SMc(C_gcv, 16)[:, g * 2:g * 2 + 2]))
        em.fence("vector", "tensor")
        for fb in range(4):
            for g in range(G):
                pe(lambda e, fb=fb, g=g: e.matmul(
                    PS[0][:, fb * 2:fb * 2 + 2],
                    gw1[g][:, fb * 128:(fb + 1) * 128], gcb[g],
                    start=(fb == 0 and g == 0), stop=False,
                    skip_group_check=True))
        em.fence("tensor", "scalar")
        for fb in range(4):
            act(lambda e, fb=fb: e.activation(
                h1g[fb], PS[0][:, fb * 2:fb * 2 + 2], AF.Gelu,
                bias=SMc(C_gcb1, 4)[:, fb:fb + 1]))
        em.fence("scalar", "tensor")
        for fb in range(2):
            for k in range(4):
                pe(lambda e, fb=fb, k=k: e.matmul(
                    PS[1][:, fb * 2:fb * 2 + 2],
                    gw2[k][:, fb * 128:(fb + 1) * 128], h1g[k],
                    start=(fb == 0 and k == 0), stop=False,
                    skip_group_check=True))
        em.fence("tensor", "scalar")
        for fb in range(2):
            act(lambda e, fb=fb: e.activation(
                h2g[fb], PS[1][:, fb * 2:fb * 2 + 2], AF.Gelu,
                bias=SMc(C_gcb2, 2)[:, fb:fb + 1]))
        em.fence("scalar", "tensor")
        for k in range(2):
            pe(lambda e, k=k: e.matmul(PS[2][0:1, 40:42], gw3[k], h2g[k],
                                       start=(k == 0), stop=(k == 1)))
        em.fence("tensor", "scalar")
        act(lambda e: e.activation(SMc(C_gf, 2, 0, 1), PS[2][0:1, 40:42],
                                   AF.Sigmoid,
                                   bias=SMc(C_sc3, 3, 0, 1)[0:1, 1:2]))
        em.fence("scalar", "vector")
        arv = SMc(C_sc3, 3, 0, 1)[0:1, 2:3]
        dve(lambda e: e.tensor_scalar(SMc(C_kb, 4, 0, 1)[0:1, 0:2],
                                      SMc(C_gf, 2, 0, 1), arv, None, OP.mult))
        em.sfence("vector")
        dve(lambda e: e.tensor_scalar(SMc(C_kb, 4, 0, 1)[0:1, 0:2],
                                      SMc(C_kb, 4, 0, 1)[0:1, 0:2], 1.4, None,
                                      OP.mult))
        dve(lambda e: e.tensor_scalar(SMc(C_tmp, 4, 0, 1)[0:1, 0:1], arv, -1.0,
                                      None, OP.mult))
        em.sfence("vector")
        dve(lambda e: e.tensor_scalar(SMc(C_tmp, 4, 0, 1)[0:1, 0:1],
                                      SMc(C_tmp, 4, 0, 1)[0:1, 0:1], 0.1, None,
                                      OP.add))
        em.sfence("vector")
        dve(lambda e: e.tensor_tensor(
            SMc(C_kb, 4, 0, 1)[0:1, 0:2], SMc(C_kb, 4, 0, 1)[0:1, 0:2],
            SMc(C_tmp, 4, 0, 1)[0:1, 0:1].to_broadcast([1, 2]), op=OP.add))
        dve(lambda e: e.tensor_scalar(SMc(C_kb, 4, 0, 1)[0:1, 2:3], arv, 0.6,
                                      None, OP.mult))
        em.barrier()
        ar.reset(base)
        base = ar.save()
        w1 = [ar.bf16(1024) for _ in range(G)]
        w2 = [ar.bf16(512) for _ in range(G)]
        w3 = [ar.bf16(256) for _ in range(4)]
        w4 = [ar.bf16(1) for _ in range(2)]
        wstg_d = ar.f32(1024)
        xlf2 = [[ar.bf16(SLAB) for _ in range(G)] for _ in range(2)]
        obf = [[ar.bf16(SLAB) for _ in range(G)] for _ in range(2)]
        Dl = [ar.f32(SLAB) for _ in range(G)]
        xsl_d = [ar.f32(SLAB) for _ in range(G)]
        h1 = [ar.bf16(SLAB) for _ in range(G)]
        h2 = [ar.bf16(SLAB) for _ in range(4)]
        h3 = [ar.bf16(SLAB) for _ in range(2)]
        tfrow = ar.f32(SLAB, parts=1)
        crow = ar.f32(SLAB, parts=1)
        chi = ar.bf16(SLAB, parts=1)
        clo = ar.bf16(SLAB, parts=1)
        bCb = ar.bf16(SLAB)
        bCnb = ar.bf16(SLAB)
        tmbl = [ar.bf16(SLAB) for _ in range(G)]
        tmpf = [ar.f32(SLAB) for _ in range(G)]
        bA_d = ar.f32(SLAB)
        for g in range(G):
            dma_in(wstg_d, cn_w1[g * 128:(g + 1) * 128, :])
            em.fence("sync", "vector")
            dve(lambda e, g=g: e.tensor_copy(w1[g], wstg_d))
            em.fence("vector", "sync")
            dma_in(wstg_d[:, 0:512], cn_w2[g * 128:(g + 1) * 128, :])
            em.fence("sync", "vector")
            dve(lambda e, g=g: e.tensor_copy(w2[g], wstg_d[:, 0:512]))
            em.fence("vector", "sync")
        for k in range(4):
            dma_in(wstg_d[:, 0:256], cn_w3[k * 128:(k + 1) * 128, :])
            em.fence("sync", "vector")
            dve(lambda e, k=k: e.tensor_copy(w3[k], wstg_d[:, 0:256]))
            em.fence("vector", "sync")
        for k in range(2):
            dma_in(wstg_d[:, 0:1], cn_w4[k * 128:(k + 1) * 128, :])
            em.fence("sync", "vector")
            dve(lambda e, k=k: e.tensor_copy(w4[k], wstg_d[:, 0:1]))
            em.fence("vector", "sync")
        em.barrier()
        twist_src = [4, 5, 6, 7, 0, 1, 2, 3]
        twist_neg = [False, False, True, True, True, True, False, False]

        def fd_load(s, rdm=None):
            if rdm is not None:
                em.wait("sync", "vector", rdm[0])
                em.wait("sync", "tensor", rdm[1])
            cl = s * SLAB
            for g in range(G):
                dma_in(xlf2[s % 2][g], h_xlf[g * 128:(g + 1) * 128, cl:cl + SLAB])
            return em.mark("sync")

        fd_ld = {0: fd_load(0)}
        fd_rdm = {}
        for s in range(NSLAB):
            c0 = s * SLAB
            b = s // SPB
            xlf = xlf2[s % 2]
            if s + 1 < NSLAB:
                fd_ld[s + 1] = fd_load(s + 1, fd_rdm.get(s - 1))
            if s > 0:
                em.wait("sync", "vector", fd_rdm[s - 1][0])
                em.wait("sync", "gpdma", fd_rdm[s - 1][2])
            for g in range(G):
                dma_in(xsl_d[g], h_x[g * 128:(g + 1) * 128, c0:c0 + SLAB])
            xs_ld = em.mark("sync")
            for dst in ("vector", "gpsimd", "tensor"):
                em.wait(dst, "sync", fd_ld[s])
            if s > 0:
                em.wait("gpsimd", "vector", fd_rdm[s - 1][0])
            for g in range(G):
                gp(lambda e, g=g: e.memset(Dl[g][:], 0.0))
            em.fence("gpsimd", "vector")
            cur = xlf
            for cyc in range(3):
                def evac1(fb, pst):
                    act(lambda e, fb=fb, pst=pst: e.activation(
                        h1[fb], pst[:], AF.Gelu,
                        bias=SMc(C_cnb1, 8)[:, fb:fb + 1]))
                proj_mm(G, G, lambda dg, fb: w1[dg][:, fb * 128:(fb + 1) * 128],
                        lambda dg, cur=cur: cur[dg], evac1)
                em.fence("scalar", "tensor")

                def evac2(fb, pst):
                    act(lambda e, fb=fb, pst=pst: e.activation(
                        h2[fb], pst[:], AF.Gelu,
                        bias=SMc(C_cnb2, 4)[:, fb:fb + 1]))
                proj_mm(4, G, lambda kg, fb: w2[kg][:, fb * 128:(fb + 1) * 128],
                        lambda kg: h1[kg], evac2)
                em.fence("scalar", "tensor")

                def evac3(fb, pst):
                    act(lambda e, fb=fb, pst=pst: e.activation(
                        h3[fb], pst[:], AF.Gelu,
                        bias=SMc(C_cnb3, 2)[:, fb:fb + 1]))
                proj_mm(2, 4, lambda kg, fb: w3[kg][:, fb * 128:(fb + 1) * 128],
                        lambda kg: h2[kg], evac3)
                em.fence("scalar", "tensor")
                for k in range(2):
                    pe(lambda e, k=k: e.matmul(PS[2][0:1, :], w4[k], h3[k],
                                               start=(k == 0), stop=(k == 1)))
                em.fence("tensor", "scalar")
                act(lambda e: e.activation(tfrow, PS[2][0:1, :], AF.Sigmoid,
                                           bias=SMc(C_sc3, 3, 0, 1)[0:1, 0:1]))
                em.fence("scalar", "vector")
                dve(lambda e, b=b: e.scalar_tensor_tensor(
                    crow, tfrow, SMc(C_kb, 4, 0, 1)[0:1, 2:3],
                    SMc(C_kb, 4, 0, 1)[0:1, b:b + 1].to_broadcast([1, SLAB]),
                    OP.mult, OP.add))
                em.sfence("vector")
                dve(lambda e: e.tensor_copy(chi, crow))
                em.sfence("vector")
                dve(lambda e: e.tensor_tensor(clo, crow, chi, op=OP.subtract))
                em.fence("vector", "tensor")
                bcast_rows(PS[3], chi, clo, 0, n=SLAB)
                bcast_rows(PS[4], chi, clo, 0, n=SLAB, neg=True)
                em.fence("tensor", "scalar")
                act(lambda e: e.copy(bCb, PS[3][:]))
                act(lambda e: e.copy(bCnb, PS[4][:]))
                em.fence("scalar", "vector")
                nxt = obf[cyc % 2]
                for g in range(G):
                    src = twist_src[g]
                    cb = bCnb if twist_neg[g] else bCb
                    dve(lambda e, g=g, src=src, cb=cb, cur=cur: e.tensor_tensor(
                        tmbl[g], cur[src], cb, op=OP.mult))
                for g in range(G):
                    dve(lambda e, g=g: e.tensor_tensor(Dl[g], Dl[g], tmbl[g],
                                                       op=OP.add))
                if cyc < 2:
                    for g in range(G):
                        dve(lambda e, g=g, nxt=nxt, xlf=xlf: e.tensor_tensor(
                            nxt[g], xlf[g], Dl[g], op=OP.add))
                em.fence("vector", "tensor")
                cur = nxt
            em.wait("vector", "sync", xs_ld)
            em.fence("scalar", "tensor")
            bcast_rows(PS[5], rows_a["Ahi"], rows_a["Alo"], c0)
            bcast_rows(PS[6], rows_a["Bhi"], rows_a["Blo"], c0)
            em.fence("tensor", "scalar")
            act(lambda e: e.copy(bA_d, PS[5][:]))
            em.fence("scalar", "vector")
            for g in range(G):
                dve(lambda e, g=g: e.tensor_tensor(tmpf[g], xsl_d[g], bA_d,
                                                   op=OP.mult))
            for g in range(G):
                dve(lambda e, g=g: e.tensor_tensor(xsl_d[g], xsl_d[g], tmpf[g],
                                                   op=OP.add))
            for g in range(G):
                em.sfence("vector")
                dve(lambda e, g=g: e.tensor_tensor(xsl_d[g], xsl_d[g], PS[6][:],
                                                   op=OP.add))
            for g in range(G):
                em.sfence("vector")
                dve(lambda e, g=g: e.tensor_tensor(xsl_d[g], xsl_d[g], Dl[g],
                                                   op=OP.add))
            for g in range(G):
                em.fence("vector", "gpsimd")
                dma_out(dst_hbm[g * 128:(g + 1) * 128, c0:c0 + SLAB], xsl_d[g])
            fd_rdm[s] = (em.mark("vector"), em.mark("tensor"),
                         em.mark("gpdma"))
        em.barrier()
        ar.reset(base)

    def pass_attn_b():
        ln_stats_pass(lambda g, c0: h_x[g * 128:(g + 1) * 128, c0:c0 + SLAB],
                      rows_a)
        base = ar.save()
        Wi = [ar.bf16(1024) for _ in range(G)]
        Wf = [ar.bf16(1024) for _ in range(G)]
        ppos_bb = ar.save()
        xsl2b = [[ar.f32(SLAB) for _ in range(G)] for _ in range(2)]
        wstg = ARN[:, ppos_bb:ppos_bb + 1024]
        xlb2r = [[ar.bf16(SLAB) for _ in range(G)] for _ in range(2)]
        tmp = [ar.f32(SLAB) for _ in range(G)]
        bAb = ar.f32(SLAB)
        bBb = ar.f32(SLAB)
        isil = [ARN[:, dtsb_pos + g * SLAB:dtsb_pos + (g + 1) * SLAB]
                for g in range(G)]
        hscr = [[ar.f32(SLAB) for _ in range(G)] for _ in range(2)]
        for g in range(G):
            dma_in(wstg, wt_i[g * 128:(g + 1) * 128, :])
            em.fence("sync", "vector")
            dve(lambda e, g=g: e.tensor_copy(Wi[g], wstg))
            em.fence("vector", "sync")
            dma_in(wstg, wt_f[g * 128:(g + 1) * 128, :])
            em.fence("sync", "vector")
            dve(lambda e, g=g: e.tensor_copy(Wf[g], wstg))
            em.fence("vector", "sync")
        em.barrier()
        def bb_load(s, vm=None):
            if vm is not None:
                em.wait("sync", "vector", vm)
            cl = s * SLAB
            for g in range(G):
                dma_in(xsl2b[s % 2][g],
                       h_x[g * 128:(g + 1) * 128, cl:cl + SLAB])
            return em.mark("sync")

        bb_ld = {0: bb_load(0)}
        bb_rdm = {}
        for s in range(NSLAB):
            c0 = s * SLAB
            seq_start = (s % SPB == 0)
            if s + 1 < NSLAB:
                bb_ld[s + 1] = bb_load(s + 1,
                                       bb_rdm[s - 1][0] if s > 0 else None)
            em.wait("vector", "sync", bb_ld[s])
            if s >= 2:
                em.wait("vector", "tensor", bb_rdm[s - 2][1])
                em.wait("vector", "gpdma", bb_rdm[s - 2][2])
            if s >= 1:
                em.wait("scalar", "vector", bb_rdm[s - 1][0])
            xsl = xsl2b[s % 2]
            xlb = xlb2r[s % 2]
            fsig = xsl
            ln_apply_emit(xsl, xlb, rows_a, c0, tmp, bAb, bBb)
            em.fence("vector", ["gpsimd", "tensor"])
            for g in range(G):
                dma_out(h_xlb[g * 128:(g + 1) * 128, c0:c0 + SLAB], xlb[g])

            if s > 0:
                em.wait("scalar", "vector", bb_scan_m)

            def evac_i(fb, pst):
                act(lambda e, fb=fb, pst=pst: e.activation(isil[fb], pst[:],
                                                           AF.Silu))
            proj_mm(G, G, lambda dg, fb: Wi[dg][:, fb * 128:(fb + 1) * 128],
                    lambda dg, xlb=xlb: xlb[dg], evac_i)

            def evac_f(fb, pst, fsig=fsig):
                act(lambda e, fb=fb, pst=pst, fsig=fsig: e.activation(
                    fsig[fb], pst[:], AF.Sigmoid))
            proj_mm(G, G, lambda dg, fb: Wf[dg][:, fb * 128:(fb + 1) * 128],
                    lambda dg, xlb=xlb: xlb[dg], evac_f)
            em.fence("scalar", "vector")
            for g in range(G):
                dve(lambda e, g=g, fsig=fsig: e.tensor_scalar(
                    tmp[g], fsig[g], -1.0, 1.0, OP.mult, OP.add))
            for g in range(G):
                dve(lambda e, g=g: e.tensor_tensor(isil[g], isil[g], tmp[g],
                                                   op=OP.mult))
            if s >= 2:
                em.wait("vector", "gpdma", bb_rdm[s - 2][2])
            for g in range(G):
                if seq_start:
                    dve(lambda e, g=g, s=s, fsig=fsig: e.tensor_tensor_scan(
                        hscr[s % 2][g], fsig[g], isil[g], 0.0, OP.mult,
                        OP.add))
                else:
                    dve(lambda e, g=g, s=s, fsig=fsig: e.tensor_tensor_scan(
                        hscr[s % 2][g], fsig[g], isil[g],
                        SMc(C_carry, 8)[:, g:g + 1], OP.mult, OP.add))
            em.sfence("vector")
            for g in range(G):
                dve(lambda e, g=g, s=s: e.tensor_copy(
                    SMc(C_carry, 8)[:, g:g + 1],
                    hscr[s % 2][g][:, SLAB - 1:SLAB]))
                em.fence("vector", "gpsimd")
                dma_out(h_hb[g * 128:(g + 1) * 128, c0:c0 + SLAB],
                        hscr[s % 2][g])
            bb_scan_m = em.mark("vector")
            bb_rdm[s] = (bb_scan_m, em.mark("tensor"), em.mark("gpdma"))
        em.barrier()
        ar.reset(base)
        base = ar.save()
        Wg = [ar.bf16(1024) for _ in range(G)]
        Wo = [ar.bf16(1024) for _ in range(G)]
        wstg_o = ar.f32(1024)
        xlb2w = [[ar.bf16(SLAB) for _ in range(G)] for _ in range(2)]
        gg = [ar.f32(SLAB) for _ in range(G)]
        hbtw = [[ar.f32(SLAB) for _ in range(G)],
                [ARN[:, dtsb_pos + g * SLAB:dtsb_pos + (g + 1) * SLAB]
                 for g in range(G)]]
        ob = [ar.bf16(SLAB) for _ in range(G)]
        xsl_o = [ar.f32(SLAB) for _ in range(G)]
        sqb = [ar.bf16(SLAB) for _ in range(2)]
        bR = ar.f32(SLAB)
        rrow = ar.f32(SLAB, parts=1)
        rhi = ar.bf16(SLAB, parts=1)
        rlo = ar.bf16(SLAB, parts=1)
        tmpf2 = [ar.f32(SLAB) for _ in range(G)]
        for g in range(G):
            dma_in(wstg_o, wt_g[g * 128:(g + 1) * 128, :])
            em.fence("sync", "vector")
            dve(lambda e, g=g: e.tensor_copy(Wg[g], wstg_o))
            em.fence("vector", "sync")
            dma_in(wstg_o, wt_o[g * 128:(g + 1) * 128, :])
            em.fence("sync", "vector")
            dve(lambda e, g=g: e.tensor_copy(Wo[g], wstg_o))
            em.fence("vector", "sync")
        em.barrier()
        def bc_load(s, rdm=None):
            if rdm is not None:
                em.wait("sync", "tensor", rdm[1])
                em.wait("sync", "scalar", rdm[2])
                em.wait("sync", "vector", rdm[0])
            cl = s * SLAB
            for g in range(G):
                dma_in(xlb2w[s % 2][g],
                       h_xlb[g * 128:(g + 1) * 128, cl:cl + SLAB])
                dma_in(hbtw[s % 2][g],
                       h_hb[g * 128:(g + 1) * 128, cl:cl + SLAB])
            return em.mark("sync")

        bc_ld = {0: bc_load(0)}
        bc_rdm = {}
        for s in range(NSLAB):
            c0 = s * SLAB
            xlb2 = xlb2w[s % 2]
            hbt = hbtw[s % 2]
            if s + 1 < NSLAB:
                bc_ld[s + 1] = bc_load(s + 1, bc_rdm.get(s - 1))
            if s > 0:
                em.wait("sync", "vector", bc_rdm[s - 1][0])
                em.wait("sync", "gpdma", bc_rdm[s - 1][3])
            for g in range(G):
                dma_in(xsl_o[g], h_x[g * 128:(g + 1) * 128, c0:c0 + SLAB])
            xo_ld = em.mark("sync")
            for dst in ("tensor", "scalar", "vector"):
                em.wait(dst, "sync", bc_ld[s])
            if s > 0:
                em.wait("scalar", "vector", bc_rdm[s - 1][0])

            def evac_g(fb, pst):
                act(lambda e, fb=fb, pst=pst: e.activation(gg[fb], pst[:],
                                                           AF.Silu))
            proj_mm(G, G, lambda dg, fb: Wg[dg][:, fb * 128:(fb + 1) * 128],
                    lambda dg, xlb2=xlb2: xlb2[dg], evac_g)
            for g in range(G):
                if g >= 2:
                    em.wait("scalar", "tensor")
                act(lambda e, g=g, hbt=hbt: e.activation(sqb[g % 2], hbt[g],
                                                         AF.Square))
                em.fence("scalar", "tensor")
                pe(lambda e, g=g: e.matmul(PS[2][0:1, :], ONESC[:, 2:3],
                                           sqb[g % 2], start=(g == 0),
                                           stop=(g == G - 1)))
            em.fence("tensor", "scalar")
            act(lambda e: e.copy(rrow, PS[2][0:1, :]))
            em.fence("scalar", "vector")
            dve(lambda e: e.tensor_scalar(rrow, rrow, 1.0 / D, None, OP.mult))
            em.fence("vector", "scalar")
            act(lambda e: e.activation(rrow, rrow, AF.Sqrt, bias=SMALL[0:1, C_eps:C_eps + 1]))
            em.fence("scalar", "vector")
            dve(lambda e: e.reciprocal(rrow, rrow))
            em.sfence("vector")
            dve(lambda e: e.tensor_copy(rhi, rrow))
            em.sfence("vector")
            dve(lambda e: e.tensor_tensor(rlo, rrow, rhi, op=OP.subtract))
            em.fence("vector", "tensor")
            em.fence("scalar", "tensor")
            bcast_rows(PS[4], rhi, rlo, 0, n=SLAB)
            em.fence("tensor", "scalar")
            act(lambda e: e.copy(bR, PS[4][:]))
            em.fence("scalar", "vector")
            for g in range(G):
                dve(lambda e, g=g, hbt=hbt: e.scalar_tensor_tensor(
                    tmpf2[g], hbt[g], SMc(C_bnw, 8)[:, g:g + 1], bR, OP.mult,
                    OP.mult))
            if s > 0:
                em.wait("vector", "tensor", bc_rdm[s - 1][1])
            for g in range(G):
                dve(lambda e, g=g: e.tensor_tensor(ob[g], tmpf2[g], gg[g],
                                                   op=OP.mult))
            em.fence("vector", "tensor")
            for dg in range(G):
                pst = PS[dg % 2]
                if dg >= 2:
                    em.wait("tensor", "vector")
                for kg in range(G):
                    pe(lambda e, dg=dg, kg=kg, pst=pst: e.matmul(
                        pst[:], Wo[kg][:, dg * 128:(dg + 1) * 128], ob[kg],
                        start=(kg == 0), stop=(kg == G - 1)))
                em.fence("tensor", "vector")
                if dg == 0:
                    em.wait("vector", "sync", xo_ld)
                dve(lambda e, dg=dg, pst=pst: e.tensor_tensor(
                    xsl_o[dg], xsl_o[dg], pst[:], op=OP.add))
                em.fence("vector", "gpsimd")
                dma_out(h_x[dg * 128:(dg + 1) * 128, c0:c0 + SLAB], xsl_o[dg])
            bc_rdm[s] = (em.mark("vector"), em.mark("tensor"),
                         em.mark("scalar"), em.mark("gpdma"))
        em.barrier()
        ar.reset(base)

    ps_on = PASSES
    if "stats" in ps_on:
        pass_a_stats()
    if "AB" in ps_on:
        pass_a_AB()
    if "Z" in ps_on:
        pass_a_Z()
    if "C1" in ps_on:
        pass_a_C1()
    if "C2" in ps_on:
        pass_a_C2()
    if nsub >= 2 and "F" in ps_on:
        pass_ffn(last=False)
    if nsub >= 3 and "B" in ps_on:
        pass_attn_b()
    if nsub >= 4 and "F" in ps_on:
        pass_ffn(last=True)
    if nsub < 4:
        base = ar.save()
        t = ar.f32(SLAB)
        for s in range(NSLAB):
            for g in range(G):
                dma_in(t, h_x[g * 128:(g + 1) * 128, s * SLAB:(s + 1) * SLAB])
                em.fence("sync", "gpsimd")
                dma_out(out_fm[g * 128:(g + 1) * 128, s * SLAB:(s + 1) * SLAB], t)
                em.fence("gpdma", "sync")
        ar.reset(base)
    em.barrier()

    with nc.Block() as block:
        em.replay(block)
    es.close()
    return nc


def shard_inputs(inputs, core):
    x = np.asarray(inputs["L0_attn_a_x"])
    xs = x[2 * core:2 * core + 2]
    x_fm = np.ascontiguousarray(xs.transpose(2, 0, 1).reshape(D, TC))
    d = {
        "x_fm": x_fm,
        "wt_in": np.ascontiguousarray(np.asarray(inputs["L0_attn_a_W_in"]).T),
        "wt_out": np.ascontiguousarray(np.asarray(inputs["L0_attn_a_W_out"]).T),
        "wt_i": np.ascontiguousarray(np.asarray(inputs["L0_attn_b_Wi"]).T),
        "wt_f": np.ascontiguousarray(np.asarray(inputs["L0_attn_b_Wf"]).T),
        "wt_g": np.ascontiguousarray(np.asarray(inputs["L0_attn_b_Wg"]).T),
        "wt_o": np.ascontiguousarray(np.asarray(inputs["L0_attn_b_Wo"]).T),
        "cn_w1": np.ascontiguousarray(np.asarray(inputs["L0_ffn_cn_w1"])),
        "cn_w2": np.ascontiguousarray(np.asarray(inputs["L0_ffn_cn_w2"])),
        "cn_w3": np.ascontiguousarray(np.asarray(inputs["L0_ffn_cn_w3"])),
        "cn_w4": np.ascontiguousarray(np.asarray(inputs["L0_ffn_cn_w4"])),
        "gc_w1": np.ascontiguousarray(np.asarray(inputs["L0_ffn_gc_w1"])),
        "gc_w2": np.ascontiguousarray(np.asarray(inputs["L0_ffn_gc_w2"])),
        "gc_w3": np.ascontiguousarray(np.asarray(inputs["L0_ffn_gc_w3"])),
        "p_convw": np.ascontiguousarray(
            np.asarray(inputs["L0_attn_a_conv_w"]).reshape(GC, 128, 4)
            .transpose(1, 0, 2).reshape(128, 72)),
        "p_convb": np.ascontiguousarray(
            np.asarray(inputs["L0_attn_a_conv_b"]).reshape(GC, 128).T),
        "p_ndtb": np.ascontiguousarray(
            -np.asarray(inputs["L0_attn_a_dt_bias"]).reshape(NH, 1)),
        "p_alog": np.ascontiguousarray(
            np.asarray(inputs["L0_attn_a_A_log"]).reshape(NH, 1)),
        "p_dskrow": np.ascontiguousarray(
            np.repeat(np.asarray(inputs["L0_attn_a_Dskip"]), HD).reshape(1, DI)),
        "p_anw": np.ascontiguousarray(
            np.asarray(inputs["L0_attn_a_norm_w"]).reshape(GI, 128).T),
        "p_lng": np.ascontiguousarray(
            np.asarray(inputs["L0_attn_a_ln_g"]).reshape(G, 128).T),
        "p_lnb": np.ascontiguousarray(
            np.asarray(inputs["L0_attn_a_ln_b"]).reshape(G, 128).T),
        "p_bnw": np.ascontiguousarray(
            np.asarray(inputs["L0_attn_b_norm_w"]).reshape(G, 128).T),
        "p_cnb1": np.ascontiguousarray(
            np.asarray(inputs["L0_ffn_cn_b1"]).reshape(G, 128).T),
        "p_cnb2": np.ascontiguousarray(
            np.asarray(inputs["L0_ffn_cn_b2"]).reshape(4, 128).T),
        "p_cnb3": np.ascontiguousarray(
            np.asarray(inputs["L0_ffn_cn_b3"]).reshape(2, 128).T),
        "p_gcb1": np.ascontiguousarray(
            np.asarray(inputs["L0_ffn_gc_b1"]).reshape(4, 128).T),
        "p_gcb2": np.ascontiguousarray(
            np.asarray(inputs["L0_ffn_gc_b2"]).reshape(2, 128).T),
        "p_sc3": np.ascontiguousarray(np.stack(
            [np.asarray(inputs["L0_ffn_cn_b4"]).reshape(()),
             np.asarray(inputs["L0_ffn_gc_b3"]).reshape(()),
             np.asarray(inputs["L0_ffn_adaptive_range"]).reshape(())]
        ).reshape(1, 3).astype(np.float32)),
    }
    return {k: np.ascontiguousarray(np.asarray(v, dtype=np.float32))
            for k, v in d.items()}


_NC_CACHE = {}


def kernel(**inputs):
    nsub = 4
    if nsub not in _NC_CACHE:
        _NC_CACHE[nsub] = build_program(nsub)
    nc = _NC_CACHE[nsub]
    in_maps = [shard_inputs(inputs, c) for c in range(8)]
    res = run_bass_kernel_spmd(nc, in_maps, core_ids=list(range(8)))
    out = np.empty((16, 2048, 1024), dtype=np.float32)
    for c in range(8):
        o = np.asarray(res.results[c]["out_fm"])
        out[2 * c:2 * c + 2] = o.reshape(D, NB, T).transpose(1, 2, 0)
    return out


# version 08
# speedup vs baseline: 1.2102x; 1.2102x over previous
import numpy as np
import contextlib
import concourse.bass as bass
import concourse.mybir as mybir
from concourse.alu_op_type import AluOpType as OP
from concourse.bass_utils import run_bass_kernel_spmd

F32 = mybir.dt.float32
BF16 = mybir.dt.bfloat16
AF = mybir.ActivationFunctionType

D = 1024
G = 8
DI = 2048
GI = 16
CD = 2304
GC = 18
NH = 32
HD = 64
CH = 64
FP = 4384
NB = 2
T = 2048
TC = NB * T
SLAB = 512
NSLAB = TC // SLAB
SPB = T // SLAB
CPS = SLAB // CH
HSP = 21
TBS = SLAB // 128
EPS = 1e-5
ARENA_COLS = 50128
DBG = False
PASSES = {"stats", "AB", "Z", "C1", "C2", "F", "B"}
HACK = set()


class QOp:
    __slots__ = ("fn", "dma", "inc", "ch")

    def __init__(self, fn, dma):
        self.fn = fn
        self.dma = dma
        self.inc = None


class Emit:
    ENGINES = ("sync", "tensor", "vector", "scalar", "gpsimd")
    CHANNELS = ENGINES + ("gpdma",)

    def __init__(self):
        self.q = {e: [] for e in self.ENGINES}
        self.sem = {}
        self.cnt = {c: 0 for c in self.CHANNELS}
        self.dirty = {c: None for c in self.CHANNELS}
        self.water = {e: {} for e in self.ENGINES}

    def op(self, eng, fn, dma=False):
        qo = QOp(fn, dma)
        self.q[eng].append(qo)
        ch = "gpdma" if (eng == "gpsimd" and dma) else eng
        qo.ch = ch
        if dma:
            qo.inc = (ch, 16)
            self.cnt[ch] += 16
        else:
            self.dirty[ch] = qo
        return qo

    def force(self, ch):
        qo = self.dirty[ch]
        if qo is None:
            return
        qo.inc = (ch, 16 if qo.dma else 1)
        self.cnt[ch] += qo.inc[1]
        self.dirty[ch] = None

    def wait(self, eng, src, val=None):
        self.force(src)
        v = self.cnt[src] if val is None else val
        if v <= 0 or v <= self.water[eng].get(src, 0):
            return
        self.water[eng][src] = v
        self.q[eng].append(("wait", src, v))

    def sfence(self, eng):
        self.force(eng)
        v = self.cnt[eng]
        if v > self.water[eng].get(eng, 0):
            self.water[eng][eng] = v
            self.q[eng].append(("wait", eng, v))

    def fence(self, srcs, dsts):
        if not isinstance(srcs, (list, tuple)):
            srcs = [srcs]
        if not isinstance(dsts, (list, tuple)):
            dsts = [dsts]
        for s in srcs:
            for d in dsts:
                if d != s:
                    self.wait(d, s)

    def barrier(self):
        for c in self.CHANNELS:
            self.force(c)
        for d in self.ENGINES:
            for c in self.CHANNELS:
                if c != d:
                    self.wait(d, c)

    def mark(self, ch):
        self.force(ch)
        return self.cnt[ch]

    def replay(self, block):
        for eng in self.ENGINES:
            ops = self.q[eng]
            sems = self.sem

            def section(proxy, ops=ops, sems=sems):
                for item in ops:
                    if isinstance(item, QOp):
                        r = item.fn(proxy)
                        if item.inc is not None:
                            nm, amt = item.inc
                            r.then_inc(sems[nm], amt)
                    else:
                        _, src, v = item
                        proxy.wait_ge(sems[src], v)
            getattr(block, eng)(section)


class Arena:
    def __init__(self, tensor, start, end):
        self.t = tensor
        self.tb = tensor.bitcast(BF16)
        self.start = start
        self.end = end
        self.pos = start

    def _alloc(self, ncols_f32, align=8):
        p = (self.pos + align - 1) // align * align
        assert p + ncols_f32 <= self.end, \
            f"arena overflow: need {ncols_f32} at {p} end {self.end}"
        self.pos = p + ncols_f32
        return p

    def f32(self, ncols, parts=128, p0=0):
        p = self._alloc(ncols)
        return self.t[p0:p0 + parts, p:p + ncols]

    def bf16(self, ncols_bf16, parts=128, p0=0):
        p = self._alloc((ncols_bf16 + 1) // 2)
        return self.tb[p0:p0 + parts, 2 * p:2 * p + ncols_bf16]

    def save(self):
        return self.pos

    def reset(self, pos):
        self.pos = pos


def build_program(nsub=4):
    nc = bass.Bass("TRN2", target_bir_lowering=False, detect_race_conditions=False)

    def din(name, shape, dt=F32):
        return nc.dram_tensor(name, shape, dt, kind="ExternalInput").ap()

    def dint(name, shape, dt):
        return nc.dram_tensor(name, shape, dt, kind="Internal").ap()

    x_fm = din("x_fm", [D, TC])
    wt_in = din("wt_in", [D, FP])
    wt_out = din("wt_out", [DI, D])
    wt_i = din("wt_i", [D, D])
    wt_f = din("wt_f", [D, D])
    wt_g = din("wt_g", [D, D])
    wt_o = din("wt_o", [D, D])
    cn_w1 = din("cn_w1", [D, D])
    cn_w2 = din("cn_w2", [D, 512])
    cn_w3 = din("cn_w3", [512, 256])
    cn_w4 = din("cn_w4", [256, 1])
    gc_w1 = din("gc_w1", [D, 512])
    gc_w2 = din("gc_w2", [512, 256])
    gc_w3 = din("gc_w3", [256, 1])
    p_convw = din("p_convw", [128, 72])
    p_convb = din("p_convb", [128, 18])
    p_ndtb = din("p_ndtb", [32, 1])
    p_alog = din("p_alog", [32, 1])
    p_dskrow = din("p_dskrow", [1, DI])
    p_anw = din("p_anw", [128, 16])
    p_lng = din("p_lng", [128, 8])
    p_lnb = din("p_lnb", [128, 8])
    p_bnw = din("p_bnw", [128, 8])
    p_cnb1 = din("p_cnb1", [128, 8])
    p_cnb2 = din("p_cnb2", [128, 4])
    p_cnb3 = din("p_cnb3", [128, 2])
    p_gcb1 = din("p_gcb1", [128, 4])
    p_gcb2 = din("p_gcb2", [128, 2])
    p_sc3 = din("p_sc3", [1, 3])

    out_fm = nc.dram_tensor("out_fm", [D, TC], F32, kind="ExternalOutput").ap()

    h_x = dint("h_x", [D, TC], F32)
    h_xn = dint("h_xn", [D, TC], BF16)
    h_xbc = dint("h_xbc", [CD, TC], BF16)
    h_z = dint("h_z", [DI, TC], F32)
    h_yn = dint("h_yn", [DI, TC], BF16)
    h_xlf = dint("h_xlf", [D, TC], BF16)
    h_xlb = dint("h_xlb", [D, TC], BF16)
    h_hb = dint("h_hb", [D, TC], F32)
    if DBG:
        def dd(name, shape, dt):
            return nc.dram_tensor(name, shape, dt, kind="ExternalOutput").ap()
        hd_xs = dd("hd_xs", [CD, TC], BF16)
        hd_dt = dd("hd_dt", [32, TC], F32)
        hd_acs = dd("hd_acs", [32, TC], F32)
        hd_wst = dd("hd_wst", [32, TC], F32)
        hd_ym = dd("hd_ym", [DI, TC], F32)
        hd_s = dd("hd_s", [128, DI * NSLAB], F32)
        hd_dtr = dd("hd_dtr", [32, TC], F32)
        hd_sp = dd("hd_sp", [32, TC], F32)
        hd_mp = dd("hd_mp", [128, DI * TBS], BF16)
        hd_xhtm = dd("hd_xhtm", [128, DI * TBS], BF16)
        hd_srow = dd("hd_srow", [1, TC], F32)
        hd_qrow = dd("hd_qrow", [1, TC], F32)
        hd_arow = dd("hd_arow", [1, TC], BF16)
        hd_brow = dd("hd_brow", [1, TC], BF16)

    es = contextlib.ExitStack()
    ARN = es.enter_context(nc.sbuf_tensor("ARN", [128, ARENA_COLS], F32))
    ARNB = ARN.bitcast(BF16)
    CONB = es.enter_context(nc.sbuf_tensor("CONB", [128, 5120], BF16))
    SMALL = es.enter_context(nc.sbuf_tensor("SMALL", [128, 512], F32))
    PS = [es.enter_context(nc.psum_tensor(f"PS{i}", [128, 512], F32)) for i in range(7)]
    PT = es.enter_context(nc.psum_tensor("PT", [128, 1024], BF16))
    PTB2 = PS[2].bitcast(BF16)
    PTB3 = PS[3].bitcast(BF16)

    em = Emit()
    em.sem = {c: es.enter_context(nc.semaphore("sem_" + c)) for c in Emit.CHANNELS}

    IDN = CONB[:, 0:128]
    TRI2 = CONB[:, 128:192]
    SL2 = CONB[:, 192:320]
    ONESC = CONB[:, 320:448]
    MONES = CONB[:, 448:449]
    E2A = CONB[:, 452:516]
    I2 = CONB[:, 516:580]
    SELR = CONB[:, 580:580 + DI]
    DSKI = CONB[:, 2688:2688 + DI]

    def gp(fn):
        em.op("gpsimd", fn)

    def dve(fn):
        em.op("vector", fn)

    def act(fn):
        em.op("scalar", fn)

    def pe(fn):
        em.op("tensor", fn)

    def dma_in(dst, src):
        em.op("sync", lambda e, d=dst, s=src: e.dma_start(out=d, in_=s), dma=True)

    def dma_out(dst, src):
        em.op("gpsimd", lambda e, d=dst, s=src: e.dma_start(out=d, in_=s), dma=True)

    def dma_out_sync(dst, src):
        em.op("sync", lambda e, d=dst, s=src: e.dma_start(out=d, in_=s), dma=True)

    def SMc(c0, n, r0=0, r1=128):
        return SMALL[r0:r1, c0:c0 + n]

    C_convw, C_convb = 0, 72
    C_ndtb, C_expA = 90, 91
    C_anw, C_lng, C_lnb, C_bnw = 92, 108, 116, 124
    C_cnb1, C_cnb2, C_cnb3 = 132, 140, 144
    C_gcb1, C_gcb2 = 146, 150
    C_sc3 = 152
    C_gb = 156
    C_wbBC = 164
    C_wbZ = 183
    C_kb = 200
    C_carry = 204
    C_mp = 212
    C_gcv = 276
    C_gf = 292
    C_dtT = 300
    C_tmp = 428
    C_eps = 500
    C_one = 502

    def init_consts():
        gp(lambda e: e.memset(SMALL[:, C_eps:C_eps + 1], EPS))
        gp(lambda e: e.memset(SMALL[:, C_one:C_one + 2], 1.0))
        gp(lambda e: e.memset(ONESC[:], 1.0))
        gp(lambda e: e.memset(MONES[:], -1.0))
        gp(lambda e: e.memset(E2A[0:1, :], 1.0))
        gp(lambda e: e.memset(IDN[:], 1.0))
        em.sfence("gpsimd")
        gp(lambda e: e.affine_select(IDN[:], IDN[:], pattern=[[-1, 128]],
                                     compare_op=OP.is_equal, fill=0.0, base=0,
                                     channel_multiplier=1))
        gp(lambda e: e.memset(TRI2[:], 0.0))
        em.sfence("gpsimd")
        gp(lambda e: e.affine_select(TRI2[0:64, :], TRI2[0:64, :], pattern=[[-1, 64]],
                                     compare_op=OP.is_gt, fill=1.0, base=0,
                                     channel_multiplier=1))
        gp(lambda e: e.affine_select(TRI2[64:128, :], TRI2[64:128, :],
                                     pattern=[[-1, 64]], compare_op=OP.is_gt,
                                     fill=1.0, base=0, channel_multiplier=1))
        gp(lambda e: e.memset(I2[:], 1.0))
        em.sfence("gpsimd")
        gp(lambda e: e.affine_select(I2[0:64, :], I2[0:64, :], pattern=[[-1, 64]],
                                     compare_op=OP.is_equal, fill=0.0, base=0,
                                     channel_multiplier=1))
        gp(lambda e: e.affine_select(I2[64:128, :], I2[64:128, :], pattern=[[-1, 64]],
                                     compare_op=OP.is_equal, fill=0.0, base=0,
                                     channel_multiplier=1))
        gp(lambda e: e.memset(SL2[:], 0.0))
        gp(lambda e: e.memset(SL2[0:64, 0:64], 1.0))
        em.sfence("gpsimd")
        gp(lambda e: e.affine_select(SL2[0:64, 0:64], SL2[0:64, 0:64],
                                     pattern=[[-1, 64]], compare_op=OP.is_gt,
                                     fill=0.0, base=0, channel_multiplier=1))
        gp(lambda e: e.memset(SL2[64:128, 64:128], 1.0))
        em.sfence("gpsimd")
        gp(lambda e: e.affine_select(SL2[64:128, 64:128], SL2[64:128, 64:128],
                                     pattern=[[-1, 64]], compare_op=OP.is_gt,
                                     fill=0.0, base=0, channel_multiplier=1))
        gp(lambda e: e.memset(SELR[0:32, :], 1.0))
        em.sfence("gpsimd")
        gp(lambda e: e.affine_select(SELR[0:32, :], SELR[0:32, :],
                                     pattern=[[-1, 32], [0, 64]],
                                     compare_op=OP.is_equal, fill=0.0, base=0,
                                     channel_multiplier=1))

    def load_params(scr):
        for dst, src in [(SMc(C_convw, 72), p_convw), (SMc(C_convb, 18), p_convb),
                         (SMc(C_ndtb, 1, 0, 32), p_ndtb),
                         (SMc(C_expA, 1, 0, 32), p_alog),
                         (SMc(C_anw, 16), p_anw), (SMc(C_lng, 8), p_lng),
                         (SMc(C_lnb, 8), p_lnb), (SMc(C_bnw, 8), p_bnw),
                         (SMc(C_cnb1, 8), p_cnb1), (SMc(C_cnb2, 4), p_cnb2),
                         (SMc(C_cnb3, 2), p_cnb3), (SMc(C_gcb1, 4), p_gcb1),
                         (SMc(C_gcb2, 2), p_gcb2), (SMc(C_sc3, 3, 0, 1), p_sc3)]:
            dma_in(dst, src)
        em.fence("sync", ["scalar", "vector"])
        act(lambda e: e.activation(SMc(C_expA, 1, 0, 32), SMc(C_expA, 1, 0, 32),
                                   AF.Exp))
        dve(lambda e: e.tensor_tensor(SMc(C_gb, 8), SMc(C_lng, 8), SMc(C_lnb, 8),
                                      op=OP.mult))
        drf = scr["drf"]
        drb = scr["drb"]
        dma_in(drf, p_dskrow)
        em.fence("sync", "vector")
        dve(lambda e: e.tensor_copy(drb, drf))
        em.fence("vector", "tensor")
        for q in range(4):
            pe(lambda e, q=q: e.matmul(
                PS[6][:], ONESC[0:1, 0:1].to_broadcast([1, 128]),
                drb[0:1, q * 512:(q + 1) * 512], start=True, stop=True))
            em.fence("tensor", "vector")
            dve(lambda e, q=q: e.tensor_tensor(
                DSKI[:, q * 512:(q + 1) * 512].rearrange("p (h l) -> p h l", l=CH),
                PS[6][:].rearrange("p (h l) -> p h l", l=CH),
                I2.unsqueeze(1).to_broadcast([128, 8, CH]), op=OP.mult))
            em.fence("vector", "tensor")

    ar = Arena(ARN, 0, ARENA_COLS)
    PERM_ROWS = {}
    for pair in (("Ahi", "Alo"), ("Bhi", "Blo"), ("C1hi", "C1lo"), ("B1hi", "B1lo")):
        p = ar._alloc(TC // 2)
        PERM_ROWS[pair[0]] = ARNB[0:1, 2 * p:2 * p + TC]
        PERM_ROWS[pair[1]] = ARNB[64:65, 2 * p:2 * p + TC]
    dtsb_pos = ar.save()
    dt_sb = ar.f32(TC, parts=32)
    PERM_BASE = ar.save()

    def bcast_rows(ps_tile, hi_row, lo_row, c0, n=SLAB, accumulate=False,
                   neg=False, stop=True, mhalf=None):
        src_col = MONES if neg else ONESC
        M = 128 if mhalf is None else 64
        o = ps_tile[:, 0:n] if mhalf is None else ps_tile[mhalf:mhalf + 64, 0:n]
        kw = {} if mhalf is None else {"tile_position": (0, mhalf)}
        rows_seq = ((hi_row, not accumulate, stop),)
        for row, st_, sp_ in rows_seq:
            bp = row.base_partition()
            lh = src_col[bp:bp + 1, 0:1].to_broadcast([1, M])
            pe(lambda e, row=row, lh=lh, st_=st_, sp_=sp_: e.matmul(
                o, lh, row[0:1, c0:c0 + n], start=st_, stop=sp_,
                skip_group_check=True, **kw))

    def ln_stats_pass(xsrc, rows, inner=False, spill_x=None):
        base = ar.save()
        xslw = [[ar.f32(SLAB) for _ in range(G)] for _ in range(2)]
        xbf = [ar.bf16(SLAB) for _ in range(G)]
        sqb = [ar.f32(SLAB) for _ in range(G)]
        srow = ar.f32(TC, parts=1)
        qrow = ar.f32(TC, parts=1)
        smt = [ar.f32(32) for _ in range(10)]

        def ls_load(s, rdm=None):
            if rdm is not None:
                em.wait("sync", "vector", rdm[0])
                em.wait("sync", "scalar", rdm[1])
                if spill_x is not None:
                    em.wait("sync", "gpdma", rdm[2])
            for g in range(G):
                dma_in(xslw[s % 2][g], xsrc(g, s * SLAB))
            return em.mark("sync")

        ls_ld = {0: ls_load(0)}
        ls_rdm = {}
        for s in range(NSLAB):
            c0 = s * SLAB
            xsl = xslw[s % 2]
            if s + 1 < NSLAB:
                ls_ld[s + 1] = ls_load(s + 1, ls_rdm.get(s - 1))
            em.wait("vector", "sync", ls_ld[s])
            em.wait("scalar", "sync", ls_ld[s])
            if s > 0:
                em.wait("vector", "tensor", ls_rdm[s - 1][3])
                em.wait("scalar", "tensor", ls_rdm[s - 1][3])
            for g in range(G):
                dve(lambda e, g=g, xsl=xsl: e.tensor_copy(xbf[g], xsl[g]))
                act(lambda e, g=g, xsl=xsl: e.activation(sqb[g], xsl[g],
                                                         AF.Square))
            em.fence(["vector", "scalar"], "tensor")
            if s > 0:
                em.wait("tensor", "scalar", ls_rdm[s - 1][1])
            for g in range(G):
                pe(lambda e, g=g: e.matmul(PS[5][0:1, :], ONESC[:, 0:1], xbf[g],
                                           start=(g == 0), stop=(g == G - 1)))
            for g in range(G):
                pe(lambda e, g=g: e.matmul(PS[6][0:1, :],
                                           SMALL[:, C_one:C_one + 1], sqb[g],
                                           start=(g == 0), stop=(g == G - 1)))
            em.fence("tensor", "scalar")
            act(lambda e, c0=c0: e.copy(srow[0:1, c0:c0 + SLAB], PS[5][0:1, :]))
            act(lambda e, c0=c0: e.copy(qrow[0:1, c0:c0 + SLAB], PS[6][0:1, :]))
            if spill_x is not None:
                em.fence("sync", "gpsimd")
                for g in range(G):
                    dma_out(spill_x(g, c0), xsl[g])
            ls_rdm[s] = (em.mark("vector"), em.mark("scalar"),
                         em.mark("gpdma"), em.mark("tensor"))
        em.fence("scalar", "sync")
        dma_in(smt[0], srow)
        dma_in(smt[1], qrow)
        em.fence("sync", ["vector", "scalar"])
        m, t1, var, r1, Af, Bf = smt[2], smt[3], smt[4], smt[5], smt[6], smt[7]
        dve(lambda e: e.tensor_scalar(m, smt[0], 1.0 / D, None, OP.mult))
        em.sfence("vector")
        dve(lambda e: e.tensor_tensor(t1, m, m, op=OP.mult))
        em.sfence("vector")
        dve(lambda e: e.scalar_tensor_tensor(var, smt[1], 1.0 / D, t1, OP.mult,
                                             OP.subtract))
        em.fence("vector", "scalar")
        act(lambda e: e.activation(r1, var, AF.Sqrt, bias=SMALL[0:128, C_eps:C_eps + 1]))
        em.fence("scalar", "vector")
        dve(lambda e: e.reciprocal(r1, r1))
        em.sfence("vector")
        if inner:
            dve(lambda e: e.tensor_tensor(t1, r1, r1, op=OP.mult))
            em.sfence("vector")
            dve(lambda e: e.tensor_tensor(t1, var, t1, op=OP.mult))
            em.fence("vector", "scalar")
            act(lambda e: e.activation(t1, t1, AF.Sqrt, bias=SMALL[0:128, C_eps:C_eps + 1]))
            em.fence("scalar", "vector")
            dve(lambda e: e.reciprocal(t1, t1))
            em.sfence("vector")
            dve(lambda e: e.tensor_tensor(Af, r1, t1, op=OP.mult))
            em.sfence("vector")
        else:
            Af = r1
        dve(lambda e: e.scalar_tensor_tensor(Bf, m, -1.0, Af, OP.mult, OP.mult))
        em.sfence("vector")
        HIA, LOA, HIB, LOB = (ar.bf16(32) for _ in range(4))
        dve(lambda e: e.tensor_copy(HIA, Af))
        em.sfence("vector")
        dve(lambda e: e.tensor_tensor(LOA, Af, HIA, op=OP.subtract))
        dve(lambda e: e.tensor_copy(HIB, Bf))
        em.sfence("vector")
        dve(lambda e: e.tensor_tensor(LOB, Bf, HIB, op=OP.subtract))
        em.fence("vector", "sync")
        dma_in(rows["Ahi"], HIA)
        dma_in(rows["Alo"], LOA)
        dma_in(rows["Bhi"], HIB)
        dma_in(rows["Blo"], LOB)
        if DBG:
            em.fence("sync", "gpsimd")
            dma_out(hd_srow, srow)
            dma_out(hd_qrow, qrow)
            dma_out(hd_arow, rows["Ahi"])
            dma_out(hd_brow, rows["Bhi"])
        if inner:
            C1f, B1f = smt[8], smt[9]
            dve(lambda e: e.tensor_scalar(C1f, r1, 1.0, None, OP.add))
            dve(lambda e: e.scalar_tensor_tensor(B1f, m, -1.0, r1, OP.mult, OP.mult))
            em.sfence("vector")
            HI2, LO2, HI3, LO3 = (ar.bf16(32) for _ in range(4))
            dve(lambda e: e.tensor_copy(HI2, C1f))
            dve(lambda e: e.tensor_copy(HI3, B1f))
            em.sfence("vector")
            dve(lambda e: e.tensor_tensor(LO2, C1f, HI2, op=OP.subtract))
            dve(lambda e: e.tensor_tensor(LO3, B1f, HI3, op=OP.subtract))
            em.fence("vector", "sync")
            dma_in(rows["C1hi"], HI2)
            dma_in(rows["C1lo"], LO2)
            dma_in(rows["B1hi"], HI3)
            dma_in(rows["B1lo"], LO3)
        em.barrier()
        ar.reset(base)

    def ln_apply_emit(xsl, dst_bf, rows, c0, tmp, bA, bB):
        em.fence("scalar", "tensor")
        bcast_rows(PS[5], rows["Ahi"], rows["Alo"], c0)
        bcast_rows(PS[6], rows["Bhi"], rows["Blo"], c0)
        em.fence("tensor", "scalar")
        act(lambda e: e.copy(bA, PS[5][:]))
        act(lambda e: e.copy(bB, PS[6][:]))
        em.fence("scalar", ["vector", "tensor"])
        for g in range(G):
            dve(lambda e, g=g: e.tensor_tensor(tmp[g], xsl[g], bA, op=OP.mult))
        for g in range(G):
            dve(lambda e, g=g: e.tensor_tensor(dst_bf[g], tmp[g], bB, op=OP.add))

    def stage_weights(stg, items):
        ld = {}
        rd = {}
        for i, (dst, srcap, wd) in enumerate(items):
            if i >= 2:
                em.wait("sync", "vector", rd[i - 2])
            dma_in(stg[i % 2][:, 0:wd], srcap)
            ld[i] = em.mark("sync")
            em.wait("vector", "sync", ld[i])
            dve(lambda e, dst=dst, st=stg[i % 2], wd=wd:
                e.tensor_copy(dst, st[:, 0:wd]))
            rd[i] = em.mark("vector")

    proj_prev = {0: 0, 1: 0}
    def proj_mm(nfb, ndg, W, rhs, evac, mr=None):
        evac_ct = dict(proj_prev)
        for fb in range(nfb):
            pst = PS[fb % 2]
            m = 128 if mr is None else mr(fb)
            em.wait("tensor", "scalar", evac_ct[fb - 2] if fb >= 2
                    else proj_prev[fb])
            for dg in range(ndg):
                pe(lambda e, fb=fb, dg=dg, pst=pst, m=m: e.matmul(
                    pst[0:m, :], W(dg, fb), rhs(dg), start=(dg == 0),
                    stop=(dg == ndg - 1)))
            em.fence("tensor", "scalar")
            evac(fb, pst)
            evac_ct[fb] = em.mark("scalar")
            proj_prev[fb % 2] = evac_ct[fb]

    scr0 = {}
    base0 = ar.save()
    scr0["drf"] = ar.f32(DI, parts=1)
    scr0["drb"] = ar.bf16(DI, parts=1)
    init_consts()
    em.fence("gpsimd", ["vector", "scalar", "tensor"])
    load_params(scr0)
    em.barrier()
    ar.reset(base0)

    rows_a = PERM_ROWS

    def pass_a_stats():
        ln_stats_pass(lambda g, c0: x_fm[g * 128:(g + 1) * 128, c0:c0 + SLAB],
                      rows_a, inner=True,
                      spill_x=lambda g, c0: h_x[g * 128:(g + 1) * 128, c0:c0 + SLAB])

    def pass_a_AB():
        base = ar.save()
        WBC = [ar.bf16(2336) for _ in range(G)]
        wstg = ar.f32(2336)
        xslw = [[ar.f32(SLAB) for _ in range(G)] for _ in range(2)]
        xnw = [[ar.bf16(SLAB) for _ in range(G)] for _ in range(2)]
        tmp = [ar.f32(SLAB) for _ in range(G)]
        bA = ar.f32(SLAB)
        bB = ar.f32(SLAB)
        stg = [ar.bf16(SLAB) for _ in range(3)]
        for g in range(G):
            dma_in(wstg, wt_in[g * 128:(g + 1) * 128, 2048:4384])
            em.fence("sync", ["vector", "tensor"])
            dve(lambda e, g=g: e.tensor_scalar(WBC[g], wstg,
                                               SMc(C_lng, 8)[:, g:g + 1],
                                               None, OP.mult))
            for fb in range(19 if "no_wb" not in HACK else 0):
                mr = 128 if fb < 18 else 32
                pe(lambda e, g=g, fb=fb, mr=mr: e.matmul(
                    PS[2][0:mr, fb:fb + 1], wstg[:, fb * 128:fb * 128 + mr],
                    SMc(C_lnb, 8)[:, g:g + 1], start=(g == 0 and fb == 0),
                    stop=(g == G - 1 and fb == 18), skip_group_check=True))
            em.fence(["vector", "tensor"], "sync")
        em.fence("tensor", "scalar")
        if "no_wb" in HACK:
            gp(lambda e: e.memset(SMc(C_wbBC, 19), 0.0))
        else:
            act(lambda e: e.copy(SMc(C_wbBC, 18), PS[2][:, 0:18]))
            act(lambda e: e.copy(SMALL[0:32, C_wbBC + 18:C_wbBC + 19],
                                 PS[2][0:32, 18:19]))
        em.barrier()

        def ab_load(s, vm=None):
            if vm is not None:
                em.wait("sync", "vector", vm)
            for g in range(G):
                dma_in(xslw[s % 2][g],
                       x_fm[g * 128:(g + 1) * 128, s * SLAB:(s + 1) * SLAB])
            return em.mark("sync")

        ab_ld = {0: ab_load(0)}
        ab_rdm = {}
        for s in range(NSLAB):
            c0 = s * SLAB
            xsl = xslw[s % 2]
            xn = xnw[s % 2]
            if s + 1 < NSLAB:
                ab_ld[s + 1] = ab_load(s + 1,
                                       ab_rdm[s - 1][0] if s > 0 else None)
            em.wait("vector", "sync", ab_ld[s])
            if s >= 2:
                em.wait("vector", "tensor", ab_rdm[s - 2][1])
                em.wait("vector", "gpdma", ab_rdm[s - 2][2])
            if s >= 1:
                em.wait("scalar", "vector", ab_rdm[s - 1][0])
            ln_apply_emit(xsl, xn, rows_a, c0, tmp, bA, bB)
            em.fence("vector", ["gpsimd", "tensor"])
            for g in range(G):
                dma_out(h_xn[g * 128:(g + 1) * 128, c0:c0 + SLAB], xn[g])

            def evac(fb, pst, c0=c0):
                if fb < 18:
                    st_t = stg[fb % 3]
                    em.wait("scalar", "gpdma")
                    act(lambda e, fb=fb, pst=pst, st_t=st_t: e.activation(
                        st_t, pst[:], AF.Identity,
                        bias=SMc(C_wbBC, 19)[:, fb:fb + 1]))
                    em.fence("scalar", "gpsimd")
                    dma_out(h_xbc[fb * 128:(fb + 1) * 128, c0:c0 + SLAB], st_t)
                else:
                    act(lambda e, pst=pst: e.activation(
                        dt_sb[:, c0:c0 + SLAB], pst[0:32, :], AF.Identity,
                        bias=SMc(C_wbBC, 19)[0:32, 18:19]))
            proj_mm(19, G,
                    lambda dg, fb: WBC[dg][:, fb * 128:fb * 128 +
                                           (128 if fb < 18 else 32)],
                    lambda dg, xn=xn: xn[dg], evac,
                    mr=lambda fb: 128 if fb < 18 else 32)
            ab_rdm[s] = (em.mark("vector"), em.mark("tensor"),
                         em.mark("gpdma"))
        em.barrier()
        ar.reset(base)

    def pass_a_Z():
        base = ar.save()
        WZ = [ar.bf16(2048) for _ in range(G)]
        wstg = ar.f32(2048)
        xnb = [[ar.bf16(SLAB) for _ in range(G)] for _ in range(2)]
        zst = [ar.f32(SLAB) for _ in range(3)]
        for g in range(G):
            dma_in(wstg, wt_in[g * 128:(g + 1) * 128, 0:2048])
            em.fence("sync", ["vector", "tensor"])
            dve(lambda e, g=g: e.tensor_scalar(WZ[g], wstg,
                                               SMc(C_lng, 8)[:, g:g + 1],
                                               None, OP.mult))
            for fb in range(16 if "no_wb" not in HACK else 0):
                pe(lambda e, g=g, fb=fb: e.matmul(
                    PS[2][:, 19 + fb:20 + fb], wstg[:, fb * 128:(fb + 1) * 128],
                    SMc(C_lnb, 8)[:, g:g + 1], start=(g == 0 and fb == 0),
                    stop=(g == G - 1 and fb == 15), skip_group_check=True))
            em.fence(["vector", "tensor"], "sync")
        em.fence("tensor", "scalar")
        if "no_wb" in HACK:
            gp(lambda e: e.memset(SMc(C_wbZ, 16), 0.0))
        else:
            act(lambda e: e.copy(SMc(C_wbZ, 16), PS[2][:, 19:35]))
        em.barrier()

        def _z_slab(s, c0, cur):
            def evac(fb, pst, c0=c0):
                zt = zst[fb % 3]
                em.wait("scalar", "gpdma")
                act(lambda e, pst=pst, zt=zt, fb=fb: e.activation(
                    zt, pst[:], AF.Identity, bias=SMc(C_wbZ, 16)[:, fb:fb + 1]))
                em.fence("scalar", "gpsimd")
                dma_out(h_z[fb * 128:(fb + 1) * 128, c0:c0 + SLAB], zt)
            proj_mm(16, G, lambda dg, fb: WZ[dg][:, fb * 128:(fb + 1) * 128],
                    lambda dg, cur=cur: cur[dg], evac)
        if True:
            for s in range(NSLAB):
                c0 = s * SLAB
                cur = xnb[0]
                em.fence("tensor", "sync")
                for g in range(G):
                    dma_in(cur[g], h_xn[g * 128:(g + 1) * 128, c0:c0 + SLAB])
                em.fence("sync", "tensor")
                _z_slab(s, c0, cur)
            em.barrier()
            ar.reset(base)
            return
        for g in range(G):
            dma_in(xnb[0][g], h_xn[g * 128:(g + 1) * 128, 0:SLAB])
        for s in range(NSLAB):
            c0 = s * SLAB
            cur = xnb[s % 2]
            if s + 1 < NSLAB:
                if s >= 1:
                    em.wait("sync", "tensor")
                for g in range(G):
                    dma_in(xnb[(s + 1) % 2][g],
                           h_xn[g * 128:(g + 1) * 128, c0 + SLAB:c0 + 2 * SLAB])
            em.fence("sync", "tensor")

            _z_slab(s, c0, cur)
        em.barrier()
        ar.reset(base)

    def pass_a_C1():
        base = ar.save()
        diag = [ar.bf16(128) for _ in range(72)]
        blk = ar._alloc(GC * 260 + GC * 256)
        xbcin = [ARNB[:, 2 * (blk + g * 260):2 * (blk + g * 260) + 520]
                 for g in range(GC)]
        xsb = blk + GC * 260
        xs = [ARNB[:, 2 * (xsb + g * 256):2 * (xsb + g * 256) + SLAB]
              for g in range(GC)]
        YM = ARN[:, blk:blk + GI * SLAB]
        assert xsb + 16 * 256 >= blk + GI * SLAB, "Bm/Cm overlap YM alias"
        xh_tm = [ar.bf16(DI) for _ in range(TBS)]
        xhd = [ar.bf16(DI) for _ in range(TBS)]
        B_tm = [ar.bf16(128) for _ in range(TBS)]
        aTri = ar.bf16(DI)
        Etile = ar.bf16(DI)
        GmS = ar.bf16(64)
        S = ar.f32(DI)
        Sbf = ar.bf16(DI)
        dtw = [ar.f32(SLAB, parts=32) for _ in range(6)]
        abf = ar.bf16(SLAB, parts=32)
        eac = ar.bf16(SLAB, parts=32)
        dtbf = ar.bf16(SLAB, parts=32)
        wstbf = ar.bf16(SLAB, parts=32)
        earow = [ar.bf16(DI, parts=1) for _ in range(2)]
        a_tm = [ar.bf16(32) for _ in range(TBS)]
        wst_tm = [ar.bf16(32) for _ in range(TBS)]
        gsm = ar.f32(CPS, parts=32)
        ghi = ar.bf16(CPS, parts=32)
        glo = ar.bf16(CPS, parts=32)
        ztile = [ar.f32(SLAB) for _ in range(2)]
        tEA = aTri[:, 0:SLAB]
        sqb = [ar.bf16(SLAB) for _ in range(2)]
        ynst = [ar.bf16(SLAB) for _ in range(2)]
        bR = ar.f32(SLAB)
        rrow = ar.f32(SLAB, parts=1)
        rhi = ar.bf16(SLAB, parts=1)
        rlo = ar.bf16(SLAB, parts=1)

        def ym(g):
            return YM[:, g * SLAB:(g + 1) * SLAB]

        em.fence("gpsimd", "vector")
        for g in range(GC):
            for k in range(4):
                dve(lambda e, g=g, k=k: e.tensor_scalar(
                    diag[g * 4 + k], IDN,
                    SMc(C_convw, 72)[:, g * 4 + k:g * 4 + k + 1], None, OP.mult))
        em.barrier()

        for s in range(NSLAB):
            c0 = s * SLAB
            seq_start = (s % SPB == 0)
            for g in range(GC):
                if seq_start:
                    gp(lambda e, g=g: e.memset(xbcin[g][:, 0:3], 0.0))
                    dma_in(xbcin[g][:, 3:515],
                           h_xbc[g * 128:(g + 1) * 128, c0:c0 + SLAB])
                else:
                    dma_in(xbcin[g][:, 0:515],
                           h_xbc[g * 128:(g + 1) * 128, c0 - 3:c0 + SLAB])
            em.fence(["sync", "gpsimd"], "tensor")
            ea_dma = {}
            ea_rd = {}

            def dt_seg_a():
                dtr = dt_sb[:, c0:c0 + SLAB]
                act(lambda e, dtr=dtr: e.activation(dtw[0][:], dtr, AF.Sigmoid,
                                                    bias=SMc(C_ndtb, 1, 0, 32),
                                                    scale=-1.0))
                em.sfence("scalar")
                act(lambda e: e.activation(dtw[0][:], dtw[0][:], AF.Ln))
                em.fence("scalar", "vector")
                dve(lambda e: e.tensor_scalar(dtw[1][:], dtw[0][:], -1.0, None,
                                              OP.mult))
                dve(lambda e: e.tensor_scalar(dtw[2][:], dtw[0][:],
                                              SMc(C_expA, 1, 0, 32), None,
                                              OP.mult))
                em.sfence("vector")
                dve(lambda e: e.tensor_copy(abf, dtw[2][:]))
                dve(lambda e: e.tensor_copy(dtbf, dtw[1][:]))
                for ci in range(CPS):
                    dve(lambda e, ci=ci: e.tensor_tensor_scan(
                        dtw[3][:, ci * CH:(ci + 1) * CH], ONESC[0:32, 0:CH],
                        dtw[2][:, ci * CH:(ci + 1) * CH], 0.0, OP.mult, OP.add))

            def dt_seg_b():
                em.fence("vector", "scalar")
                act(lambda e: e.activation(gsm[:], dtw[3][:, CH - 1::CH], AF.Exp))
                act(lambda e: e.activation(eac[:], dtw[3][:], AF.Exp))
                em.fence("scalar", "sync")
                dma_in(earow[0][0:1, :], eac[:, 0:CH])
                ea_dma[0] = em.mark("sync")
                em.fence("scalar", "vector")
                em.sfence("vector")
                dve(lambda e: e.tensor_tensor(
                    dtw[4][:].rearrange("p (c l) -> p c l", l=CH),
                    dtw[3][:, CH - 1::CH].unsqueeze(2).to_broadcast(
                        [32, CPS, CH]),
                    dtw[3][:].rearrange("p (c l) -> p c l", l=CH),
                    op=OP.subtract))

            def dt_seg_c():
                em.fence("vector", "scalar")
                act(lambda e: e.activation(dtw[4][:], dtw[4][:], AF.Exp))
                em.fence("scalar", "vector")
                dve(lambda e: e.tensor_tensor(dtw[5][:], dtw[1][:], dtw[4][:],
                                              op=OP.mult))
                dve(lambda e: e.tensor_copy(ghi[:], gsm[:]))
                em.sfence("vector")
                dve(lambda e: e.tensor_copy(wstbf, dtw[5][:]))
                dve(lambda e: e.tensor_tensor(glo[:], gsm[:], ghi[:],
                                              op=OP.subtract))

            dtsegs = {0: dt_seg_a, 6: dt_seg_b, 11: dt_seg_c}
            cbanks = (PS[0], PS[1], PS[5], PS[6])
            evac_ct = {}
            for g in range(GC):
                if g in dtsegs:
                    dtsegs[g]()
                pst = cbanks[g % 4]
                if g >= 4:
                    em.wait("tensor", "scalar", evac_ct[g - 4])
                for k in range(4):
                    pe(lambda e, g=g, k=k, pst=pst: e.matmul(
                        pst[:], diag[g * 4 + k], xbcin[g][:, k:k + SLAB],
                        start=(k == 0), stop=(k == 3)))
                em.fence("tensor", "scalar")
                act(lambda e, g=g, pst=pst: e.activation(
                    xs[g], pst[:], AF.Silu, bias=SMc(C_convb, 18)[:, g:g + 1]))
                evac_ct[g] = em.mark("scalar")
                if DBG:
                    em.fence("scalar", "gpsimd")
                    dma_out(hd_xs[g * 128:(g + 1) * 128, c0:c0 + SLAB], xs[g])
            if DBG:
                em.fence("vector", "gpsimd")
                em.fence("vector", "sync")
                em.fence("scalar", "sync")
                dma_out_sync(hd_dtr[:, c0:c0 + SLAB], dt_sb[:, c0:c0 + SLAB])
                dma_out_sync(hd_sp[:, c0:c0 + SLAB], dtw[0][:])
                dma_out_sync(hd_dt[:, c0:c0 + SLAB], dtw[1][:])
                dma_out_sync(hd_acs[:, c0:c0 + SLAB], dtw[3][:])
                dma_out_sync(hd_wst[:, c0:c0 + SLAB], dtw[5][:])
            em.fence("vector", "tensor")
            em.fence("scalar", "tensor")
            tbanks = (PT, PTB2, PTB3)
            tevac = {0: 0, 1: 0, 2: 0}
            for tb in range(TBS):
                for grp in range(3):
                    bk = (tb * 3 + grp) % 3
                    bank = tbanks[bk]
                    em.wait("tensor", "scalar", tevac[bk])
                    if grp == 0:
                        pe(lambda e, tb=tb, bank=bank: e.transpose(
                            bank[:, 0:32], abf[:, tb * 128:(tb + 1) * 128],
                            IDN[0:32, 0:32]))
                        pe(lambda e, tb=tb, bank=bank: e.transpose(
                            bank[:, 32:64], wstbf[:, tb * 128:(tb + 1) * 128],
                            IDN[0:32, 0:32]))
                        pe(lambda e, tb=tb, bank=bank: e.transpose(
                            bank[:, 64:192], xs[16][:, tb * 128:(tb + 1) * 128],
                            IDN))
                        for ci2 in range(2):
                            ci = tb * 2 + ci2
                            half = ci2 * 64
                            pe(lambda e, ci=ci, half=half, bank=bank: e.transpose(
                                bank[half:half + 64, 192:224],
                                dtbf[:, ci * CH:(ci + 1) * CH], IDN[0:32, 0:32],
                                tile_position=(0, half)))
                        mk = em.mark("tensor")
                        em.wait("scalar", "tensor", mk)
                        act(lambda e, tb=tb, bank=bank: e.copy(a_tm[tb],
                                                               bank[:, 0:32]))
                        act(lambda e, tb=tb, bank=bank: e.copy(wst_tm[tb],
                                                               bank[:, 32:64]))
                        act(lambda e, tb=tb, bank=bank: e.copy(B_tm[tb],
                                                               bank[:, 64:192]))
                        for half in (0, 64):
                            act(lambda e, tb=tb, half=half, bank=bank: e.copy(
                                SMc(C_dtT, 128)[half:half + 64,
                                                tb * 32:(tb + 1) * 32],
                                bank[half:half + 64, 192:224]))
                    else:
                        i8 = grp - 1
                        for gg in range(8):
                            pe(lambda e, tb=tb, i8=i8, gg=gg, bank=bank:
                               e.transpose(
                                   bank[:, gg * 128:(gg + 1) * 128],
                                   xs[i8 * 8 + gg][:, tb * 128:(tb + 1) * 128],
                                   IDN))
                        mk = em.mark("tensor")
                        em.wait("scalar", "tensor", mk)
                        for hf in range(2):
                            act(lambda e, tb=tb, i8=i8, hf=hf, bank=bank: e.copy(
                                xh_tm[tb][:, i8 * 1024 + hf * 512:
                                          i8 * 1024 + (hf + 1) * 512],
                                bank[:, hf * 512:(hf + 1) * 512]))
                    tevac[bk] = em.mark("scalar")
                em.wait("vector", "scalar", tevac[(tb * 3 + 2) % 3])
                dve(lambda e, tb=tb: e.tensor_tensor(
                    xhd[tb].rearrange("p (h q) -> p h q", q=HD),
                    xh_tm[tb].rearrange("p (h q) -> p h q", q=HD),
                    wst_tm[tb].unsqueeze(2).to_broadcast([128, 32, HD]),
                    op=OP.mult))
            em.fence("vector", "tensor")
            em.fence("scalar", "tensor")
            for gz in range(2):
                dma_in(ztile[gz], h_z[gz * 128:(gz + 1) * 128, c0:c0 + SLAB])
            if seq_start:
                dve(lambda e: e.memset(S[:], 0.0))
                dve(lambda e: e.memset(Sbf[:], 0.0))
                em.fence("vector", "tensor")
            em.fence("scalar", "gpsimd")
            for cp in range(TBS):
                dve(lambda e, cp=cp: e.tensor_tensor(
                    aTri.rearrange("p (h l) -> p h l", l=CH)[:, 0:HSP, :],
                    a_tm[cp].unsqueeze(2).to_broadcast([128, 32, CH])[:, 0:HSP, :],
                    TRI2.unsqueeze(1).to_broadcast([128, 32, CH])[:, 0:HSP, :],
                    op=OP.mult))
                gp(lambda e, cp=cp: e.tensor_tensor(
                    aTri.rearrange("p (h l) -> p h l", l=CH)[:, HSP:32, :],
                    a_tm[cp].unsqueeze(2).to_broadcast([128, 32, CH])[:, HSP:32, :],
                    TRI2.unsqueeze(1).to_broadcast([128, 32, CH])[:, HSP:32, :],
                    op=OP.mult))
                em.fence(["vector", "gpsimd"], "tensor")
                for q in range(4):
                    pq = PS[q % 2]
                    if q >= 2:
                        em.wait("tensor", "scalar")
                    pe(lambda e, q=q, pq=pq: e.matmul(
                        pq[:], SL2, aTri[:, q * 512:(q + 1) * 512],
                        start=True, stop=True))
                    em.fence("tensor", "scalar")
                    act(lambda e, q=q, pq=pq: e.activation(
                        Etile[:, q * 512:(q + 1) * 512], pq[:], AF.Exp))
                em.fence("scalar", ["tensor", "vector", "gpsimd"])
                for ci2 in range(2):
                    ci = cp * 2 + ci2
                    half = ci2 * 64
                    pe(lambda e, ci=ci, half=half: e.matmul(
                        PS[2][half:half + 64, 0:64],
                        xs[16][:, ci * CH:(ci + 1) * CH],
                        xs[17][:, ci * CH:(ci + 1) * CH],
                        start=True, stop=True, tile_position=(0, half),
                        skip_group_check=True))
                    em.fence("tensor", "vector")
                    dve(lambda e, half=half: e.tensor_tensor(
                        GmS[half:half + 64, :], PS[2][half:half + 64, 0:64],
                        TRI2[half:half + 64, :], op=OP.mult))
                em.sfence("vector")
                em.fence("vector", "gpsimd")
                dve(lambda e, cp=cp: e.tensor_tensor(
                    Etile.rearrange("p (h l) -> p h l", l=CH)[:, 0:HSP, :],
                    Etile.rearrange("p (h l) -> p h l", l=CH)[:, 0:HSP, :],
                    GmS.unsqueeze(1).to_broadcast([128, 32, CH])[:, 0:HSP, :],
                    op=OP.mult))
                gp(lambda e, cp=cp: e.tensor_tensor(
                    Etile.rearrange("p (h l) -> p h l", l=CH)[:, HSP:32, :],
                    Etile.rearrange("p (h l) -> p h l", l=CH)[:, HSP:32, :],
                    GmS.unsqueeze(1).to_broadcast([128, 32, CH])[:, HSP:32, :],
                    op=OP.mult))
                dve(lambda e, cp=cp: e.tensor_tensor(
                    Etile.rearrange("p (h l) -> p h l", l=CH)[:, 0:HSP, :],
                    Etile.rearrange("p (h l) -> p h l", l=CH)[:, 0:HSP, :],
                    SMc(C_dtT, 128)[:, cp * 32:(cp + 1) * 32].unsqueeze(2)
                    .to_broadcast([128, 32, CH])[:, 0:HSP, :], op=OP.mult))
                gp(lambda e, cp=cp: e.tensor_tensor(
                    Etile.rearrange("p (h l) -> p h l", l=CH)[:, HSP:32, :],
                    Etile.rearrange("p (h l) -> p h l", l=CH)[:, HSP:32, :],
                    SMc(C_dtT, 128)[:, cp * 32:(cp + 1) * 32].unsqueeze(2)
                    .to_broadcast([128, 32, CH])[:, HSP:32, :], op=OP.mult))
                dve(lambda e: e.tensor_tensor(Etile[:, 0:HSP * CH],
                                              Etile[:, 0:HSP * CH],
                                              DSKI[:, 0:HSP * CH], op=OP.add))
                gp(lambda e: e.tensor_tensor(Etile[:, HSP * CH:DI],
                                             Etile[:, HSP * CH:DI],
                                             DSKI[:, HSP * CH:DI], op=OP.add))
                if "pb1" in HACK:
                    em.barrier()
                if DBG:
                    em.fence("vector", "sync")
                    if True:
                        dma_out_sync(hd_mp[:, cp * DI:(cp + 1) * DI], Etile[:])
                        dma_out_sync(hd_xhtm[:, cp * DI:(cp + 1) * DI], xh_tm[cp][:])
                    em.fence("sync", "vector")
                em.fence(["vector", "gpsimd"], "tensor")
                for ci2 in range(2):
                    ci = cp * 2 + ci2
                    half = ci2 * 64
                    gci = s % SPB * CPS + ci
                    gc_first = (gci == 0)
                    if ci + 1 < 2 * TBS:
                        if ci - 1 in ea_rd:
                            em.wait("sync", "tensor", ea_rd[ci - 1])
                        dma_in(earow[(ci + 1) % 2][0:1, :],
                               eac[:, (ci + 1) * CH:(ci + 2) * CH])
                        ea_dma[ci + 1] = em.mark("sync")
                    em.wait("tensor", "sync", ea_dma[ci])
                    for h in range(NH):
                        tp = (half, (h % 2) * 64)
                        pst = PS[5] if h < 16 else PS[6]
                        colr = ((h // 2) % 8) * 64
                        pe(lambda e, h=h, half=half, cp=cp, pst=pst, colr=colr,
                           tp=tp: e.matmul(
                               pst[tp[1]:tp[1] + 64, colr:colr + 64],
                               xh_tm[cp][half:half + 64, h * 64:(h + 1) * 64],
                               Etile[half:half + 64, h * 64:(h + 1) * 64],
                               start=(h in (0, 1, 16, 17)),
                               stop=(h in (14, 15, 30, 31)),
                               tile_position=tp, skip_group_check=True))
                    if not gc_first:
                        for h in range(NH):
                            tp = (0, (h % 2) * 64)
                            pst = PS[0] if h < 16 else PS[1]
                            colr = ((h // 2) % 8) * 64
                            pe(lambda e, h=h, ci=ci, pst=pst, colr=colr, tp=tp:
                               e.matmul(pst[tp[1]:tp[1] + 64, colr:colr + 64],
                                        Sbf[:, h * 64:(h + 1) * 64],
                                        xs[17][:, ci * CH:(ci + 1) * CH],
                                        start=(h in (0, 1, 16, 17)),
                                        stop=(h in (14, 15, 30, 31)),
                                        tile_position=tp, skip_group_check=True))
                        for ip, pst in enumerate((PS[2], PS[4])):
                            for hh in range(2):
                                rh = earow[ci % 2].rearrange("o (h l) -> o h l",
                                                             l=CH)
                                pe(lambda e, ip=ip, hh=hh, pst=pst, rh=rh:
                                   e.matmul(
                                       pst[hh * 64:hh * 64 + 64, :],
                                       ONESC[0:1, 0:1].to_broadcast([1, 64]),
                                       rh[0:1, ip * 16 + hh:(ip + 1) * 16:2, :],
                                       start=True, stop=True,
                                       tile_position=(0, hh * 64),
                                       skip_group_check=True))
                    ea_rd[ci] = em.mark("tensor")
                    if "pb2" in HACK:
                        em.barrier()
                    em.fence("tensor", ["scalar", "vector"])
                    for ip, (pd, po, pea) in enumerate(
                            ((PS[5], PS[0], PS[2]), (PS[6], PS[1], PS[4]))):
                        act(lambda e, pd=pd, ip=ip, ci=ci: e.copy(
                            YM.rearrange("p (g t) -> p g t", t=SLAB)
                              [:, ip * 8:(ip + 1) * 8, ci * CH:(ci + 1) * CH],
                            pd[:].rearrange("p (g l) -> p g l", l=CH)))
                        if not gc_first:
                            act(lambda e, pea=pea, ip=ip: e.copy(ynst[ip], pea[:]))
                            em.fence("scalar", "vector")
                            dve(lambda e, po=po, ip=ip: e.tensor_tensor(
                                tEA, po[:], ynst[ip], op=OP.mult))
                            em.fence("vector", "vector") if False else None
                            dve(lambda e, ip=ip, ci=ci: e.tensor_tensor(
                                YM.rearrange("p (g t) -> p g t", t=SLAB)
                                  [:, ip * 8:(ip + 1) * 8, ci * CH:(ci + 1) * CH],
                                YM.rearrange("p (g t) -> p g t", t=SLAB)
                                  [:, ip * 8:(ip + 1) * 8, ci * CH:(ci + 1) * CH],
                                tEA.rearrange("p (g l) -> p g l", l=CH),
                                op=OP.add))
                    if "pb3" in HACK:
                        em.barrier()
                    em.fence(["scalar", "vector"], "tensor")
                    mm_mark = [0] * 4
                    dve_mark = [0] * 4
                    for q in range(4):
                        pb, pg = (PS[3], PS[4]) if q % 2 == 0 else (PS[5], PS[6])
                        if q >= 2:
                            em.wait("tensor", "vector", dve_mark[q - 2])
                        pe(lambda e, half=half, cp=cp, q=q, pb=pb: e.matmul(
                            pb[:], B_tm[cp][half:half + 64, :],
                            xhd[cp][half:half + 64, q * 512:(q + 1) * 512],
                            start=True, stop=True))
                        pe(lambda e, ci=ci, q=q, pg=pg: e.matmul(
                            pg[:], ghi[:, ci:ci + 1].to_broadcast([32, 128]),
                            SELR[0:32, q * 512:(q + 1) * 512],
                            start=True, stop=False, skip_group_check=True))
                        pe(lambda e, ci=ci, q=q, pg=pg: e.matmul(
                            pg[:], glo[:, ci:ci + 1].to_broadcast([32, 128]),
                            SELR[0:32, q * 512:(q + 1) * 512],
                            start=False, stop=True, skip_group_check=True))
                        mm_mark[q] = em.mark("tensor")
                        em.wait("vector", "tensor", mm_mark[q])
                        dve(lambda e, q=q, pg=pg: e.tensor_tensor(
                            S[:, q * 512:(q + 1) * 512],
                            S[:, q * 512:(q + 1) * 512], pg[:], op=OP.mult))
                        dve(lambda e, q=q, pb=pb: e.tensor_tensor(
                            S[:, q * 512:(q + 1) * 512],
                            S[:, q * 512:(q + 1) * 512], pb[:], op=OP.add))
                        dve_mark[q] = em.mark("vector")
                        em.wait("scalar", "vector", dve_mark[q])
                        act(lambda e, q=q: e.copy(Sbf[:, q * 512:(q + 1) * 512],
                                                  S[:, q * 512:(q + 1) * 512]))
                    if "pb4" in HACK:
                        em.barrier()
                    em.fence("scalar", "tensor")
                    if "pb5" in HACK:
                        em.barrier()
            if DBG:
                em.fence("scalar", "gpsimd")
                em.fence("vector", "gpsimd")
                for gdd in range(GI):
                    dma_out(hd_ym[gdd * 128:(gdd + 1) * 128, c0:c0 + SLAB],
                            YM[:, gdd * SLAB:(gdd + 1) * SLAB])
                dma_out(hd_s[:, s * DI:(s + 1) * DI], S[:])
                em.fence("gpdma", "vector")
            for g in range(GI):
                zt = ztile[g % 2]
                em.fence("sync", "scalar")
                act(lambda e, zt=zt: e.activation(zt, zt, AF.Silu))
                em.fence("scalar", "vector")
                dve(lambda e, g=g, zt=zt: e.tensor_tensor(ym(g), ym(g), zt,
                                                          op=OP.mult))
                if g + 1 < GI:
                    em.fence("vector", "sync")
                    dma_in(ztile[(g + 1) % 2],
                           h_z[(g + 1) * 128:(g + 2) * 128, c0:c0 + SLAB])
            em.fence("vector", "scalar")
            for g in range(GI):
                if g >= 2:
                    em.wait("scalar", "tensor")
                act(lambda e, g=g: e.activation(sqb[g % 2], ym(g), AF.Square))
                em.fence("scalar", "tensor")
                pe(lambda e, g=g: e.matmul(PS[2][0:1, :], ONESC[:, 2:3],
                                           sqb[g % 2], start=(g == 0),
                                           stop=(g == GI - 1)))
            em.fence("tensor", "scalar")
            act(lambda e: e.copy(rrow, PS[2][0:1, :]))
            em.fence("scalar", "vector")
            dve(lambda e: e.tensor_scalar(rrow, rrow, 1.0 / DI, None, OP.mult))
            em.fence("vector", "scalar")
            act(lambda e: e.activation(rrow, rrow, AF.Sqrt, bias=SMALL[0:1, C_eps:C_eps + 1]))
            em.fence("scalar", "vector")
            dve(lambda e: e.reciprocal(rrow, rrow))
            em.sfence("vector")
            dve(lambda e: e.tensor_copy(rhi, rrow))
            em.sfence("vector")
            dve(lambda e: e.tensor_tensor(rlo, rrow, rhi, op=OP.subtract))
            em.fence("vector", "tensor")
            bcast_rows(PS[4], rhi, rlo, 0, n=SLAB)
            em.fence("tensor", "scalar")
            act(lambda e: e.copy(bR, PS[4][:]))
            em.fence("scalar", "vector")
            for g in range(GI):
                yt = ynst[g % 2]
                if g >= 2:
                    em.wait("vector", "gpdma")
                dve(lambda e, g=g, yt=yt: e.scalar_tensor_tensor(
                    yt, ym(g), SMc(C_anw, 16)[:, g:g + 1], bR, OP.mult, OP.mult))
                em.fence("vector", "gpsimd")
                dma_out(h_yn[g * 128:(g + 1) * 128, c0:c0 + SLAB], yt)
            em.barrier()
        ar.reset(base)

    def pass_a_C2():
        base = ar.save()
        WO = [ar.bf16(1024) for _ in range(GI)]
        wstg2c = [ar.f32(1024) for _ in range(2)]
        ynw = [[ar.bf16(SLAB) for _ in range(GI)] for _ in range(2)]
        xslw2 = [[ar.f32(SLAB) for _ in range(G)] for _ in range(2)]
        bC = ar.f32(SLAB)
        stage_weights(wstg2c,
                      [(WO[kg], wt_out[kg * 128:(kg + 1) * 128, :], 1024)
                       for kg in range(GI)])
        em.barrier()

        def c2_load(s, rdm=None):
            if rdm is not None:
                em.wait("sync", "tensor", rdm[0])
                em.wait("sync", "vector", rdm[1])
                em.wait("sync", "gpdma", rdm[2])
            cl = s * SLAB
            for kg in range(GI):
                dma_in(ynw[s % 2][kg],
                       h_yn[kg * 128:(kg + 1) * 128, cl:cl + SLAB])
            for g in range(G):
                dma_in(xslw2[s % 2][g],
                       h_x[g * 128:(g + 1) * 128, cl:cl + SLAB])
            return em.mark("sync")

        c2_ld = {0: c2_load(0)}
        c2_rdm = {}
        for s in range(NSLAB):
            c0 = s * SLAB
            yn = ynw[s % 2]
            xsl = xslw2[s % 2]
            if s + 1 < NSLAB:
                c2_ld[s + 1] = c2_load(s + 1, c2_rdm.get(s - 1))
            em.fence("scalar", "tensor")
            bcast_rows(PS[4], rows_a["C1hi"], rows_a["C1lo"], c0)
            em.fence("tensor", "scalar")
            if s > 0:
                em.wait("scalar", "vector", c2_rdm[s - 1][1])
            act(lambda e: e.copy(bC, PS[4][:]))
            em.fence("scalar", "vector")
            em.wait("tensor", "sync", c2_ld[s])
            em.wait("vector", "sync", c2_ld[s])
            if s > 0:
                em.wait("tensor", "vector", c2_rdm[s - 1][1])
            for dg in range(G):
                pst = PS[dg % 2]
                if dg >= 2:
                    em.wait("tensor", "vector")
                for kg in range(GI):
                    pe(lambda e, dg=dg, kg=kg, pst=pst, yn=yn: e.matmul(
                        pst[:], WO[kg][:, dg * 128:(dg + 1) * 128], yn[kg],
                        start=(kg == 0), stop=False, skip_group_check=True))
                bcast_rows(pst, rows_a["B1hi"], rows_a["B1lo"], c0,
                           accumulate=True)
                em.fence("tensor", "vector")
                dve(lambda e, dg=dg, xsl=xsl: e.tensor_tensor(
                    xsl[dg], xsl[dg], bC, op=OP.mult))
                em.sfence("vector")
                dve(lambda e, dg=dg, pst=pst, xsl=xsl: e.tensor_tensor(
                    xsl[dg], xsl[dg], pst[:], op=OP.add))
                em.fence("vector", "gpsimd")
                dma_out(h_x[dg * 128:(dg + 1) * 128, c0:c0 + SLAB], xsl[dg])
            c2_rdm[s] = (em.mark("tensor"), em.mark("vector"),
                         em.mark("gpdma"))
        em.barrier()
        ar.reset(base)

    def pass_ffn(last=False):
        dst_hbm = out_fm if last else h_x
        ln_stats_pass(lambda g, c0: h_x[g * 128:(g + 1) * 128, c0:c0 + SLAB],
                      rows_a)
        base = ar.save()
        xsl2 = [[ar.f32(SLAB) for _ in range(G)] for _ in range(2)]
        xln2 = [[ar.bf16(SLAB) for _ in range(G)] for _ in range(2)]
        tmp = [ar.f32(SLAB) for _ in range(G)]
        bA2 = [ar.f32(SLAB) for _ in range(2)]
        bB2 = [ar.f32(SLAB) for _ in range(2)]

        def fb_load(s, vm=None):
            if vm is not None:
                em.wait("sync", "vector", vm)
            cl = s * SLAB
            for g in range(G):
                dma_in(xsl2[s % 2][g],
                       h_x[g * 128:(g + 1) * 128, cl:cl + SLAB])
            return em.mark("sync")

        fb_ld = {0: fb_load(0)}
        fb_rdm = {}
        for s in range(NSLAB):
            c0 = s * SLAB
            if s + 1 < NSLAB:
                fb_ld[s + 1] = fb_load(s + 1,
                                       fb_rdm[s - 1][0] if s > 0 else None)
            em.wait("vector", "sync", fb_ld[s])
            if s >= 2:
                em.wait("vector", "gpdma", fb_rdm[s - 2][1])
                em.wait("scalar", "vector", fb_rdm[s - 2][0])
            ln_apply_emit(xsl2[s % 2], xln2[s % 2], rows_a, c0, tmp,
                          bA2[s % 2], bB2[s % 2])
            for g in range(G):
                dve(lambda e, g=g, s=s: e.reduce_sum(
                    SMc(C_mp, 64)[:, s * 8 + g:s * 8 + g + 1], xln2[s % 2][g],
                    axis=mybir.AxisListType.X))
                em.fence("vector", "gpsimd")
                dma_out(h_xlf[g * 128:(g + 1) * 128, c0:c0 + SLAB],
                        xln2[s % 2][g])
            fb_rdm[s] = (em.mark("vector"), em.mark("gpdma"))
        em.barrier()
        ar.reset(base)
        base = ar.save()
        gw1 = [ar.bf16(512) for _ in range(G)]
        gw2 = [ar.bf16(256) for _ in range(4)]
        gw3 = [ar.bf16(1) for _ in range(2)]
        wstg_c = ar.f32(512)
        gcb = [ar.bf16(2) for _ in range(G)]
        h1g = [ar.bf16(2) for _ in range(4)]
        h2g = [ar.bf16(2) for _ in range(2)]
        for g in range(G):
            dma_in(wstg_c, gc_w1[g * 128:(g + 1) * 128, :])
            em.fence("sync", "vector")
            dve(lambda e, g=g: e.tensor_copy(gw1[g], wstg_c))
            em.fence("vector", "sync")
        for k in range(4):
            dma_in(wstg_c[:, 0:256], gc_w2[k * 128:(k + 1) * 128, :])
            em.fence("sync", "vector")
            dve(lambda e, k=k: e.tensor_copy(gw2[k], wstg_c[:, 0:256]))
            em.fence("vector", "sync")
        for k in range(2):
            dma_in(wstg_c[:, 0:1], gc_w3[k * 128:(k + 1) * 128, :])
            em.fence("sync", "vector")
            dve(lambda e, k=k: e.tensor_copy(gw3[k], wstg_c[:, 0:1]))
            em.fence("vector", "sync")
        for g in range(G):
            for b in range(NB):
                dve(lambda e, g=g, b=b: e.reduce_sum(
                    SMc(C_gcv, 16)[:, g * 2 + b:g * 2 + b + 1],
                    SMALL[:, C_mp + b * 32 + g:C_mp + b * 32 + g + 25:8],
                    axis=mybir.AxisListType.X))
        em.sfence("vector")
        dve(lambda e: e.tensor_scalar(SMc(C_gcv, 16), SMc(C_gcv, 16), 1.0 / T,
                                      None, OP.mult))
        em.sfence("vector")
        for g in range(G):
            dve(lambda e, g=g: e.tensor_copy(gcb[g],
                                             SMc(C_gcv, 16)[:, g * 2:g * 2 + 2]))
        em.fence("vector", "tensor")
        for fb in range(4):
            for g in range(G):
                pe(lambda e, fb=fb, g=g: e.matmul(
                    PS[0][:, fb * 2:fb * 2 + 2],
                    gw1[g][:, fb * 128:(fb + 1) * 128], gcb[g],
                    start=(fb == 0 and g == 0), stop=False,
                    skip_group_check=True))
        em.fence("tensor", "scalar")
        for fb in range(4):
            act(lambda e, fb=fb: e.activation(
                h1g[fb], PS[0][:, fb * 2:fb * 2 + 2], AF.Gelu,
                bias=SMc(C_gcb1, 4)[:, fb:fb + 1]))
        em.fence("scalar", "tensor")
        for fb in range(2):
            for k in range(4):
                pe(lambda e, fb=fb, k=k: e.matmul(
                    PS[1][:, fb * 2:fb * 2 + 2],
                    gw2[k][:, fb * 128:(fb + 1) * 128], h1g[k],
                    start=(fb == 0 and k == 0), stop=False,
                    skip_group_check=True))
        em.fence("tensor", "scalar")
        for fb in range(2):
            act(lambda e, fb=fb: e.activation(
                h2g[fb], PS[1][:, fb * 2:fb * 2 + 2], AF.Gelu,
                bias=SMc(C_gcb2, 2)[:, fb:fb + 1]))
        em.fence("scalar", "tensor")
        for k in range(2):
            pe(lambda e, k=k: e.matmul(PS[2][0:1, 40:42], gw3[k], h2g[k],
                                       start=(k == 0), stop=(k == 1)))
        em.fence("tensor", "scalar")
        act(lambda e: e.activation(SMc(C_gf, 2, 0, 1), PS[2][0:1, 40:42],
                                   AF.Sigmoid,
                                   bias=SMc(C_sc3, 3, 0, 1)[0:1, 1:2]))
        em.fence("scalar", "vector")
        arv = SMc(C_sc3, 3, 0, 1)[0:1, 2:3]
        dve(lambda e: e.tensor_scalar(SMc(C_kb, 4, 0, 1)[0:1, 0:2],
                                      SMc(C_gf, 2, 0, 1), arv, None, OP.mult))
        em.sfence("vector")
        dve(lambda e: e.tensor_scalar(SMc(C_kb, 4, 0, 1)[0:1, 0:2],
                                      SMc(C_kb, 4, 0, 1)[0:1, 0:2], 1.4, None,
                                      OP.mult))
        dve(lambda e: e.tensor_scalar(SMc(C_tmp, 4, 0, 1)[0:1, 0:1], arv, -1.0,
                                      None, OP.mult))
        em.sfence("vector")
        dve(lambda e: e.tensor_scalar(SMc(C_tmp, 4, 0, 1)[0:1, 0:1],
                                      SMc(C_tmp, 4, 0, 1)[0:1, 0:1], 0.1, None,
                                      OP.add))
        em.sfence("vector")
        dve(lambda e: e.tensor_tensor(
            SMc(C_kb, 4, 0, 1)[0:1, 0:2], SMc(C_kb, 4, 0, 1)[0:1, 0:2],
            SMc(C_tmp, 4, 0, 1)[0:1, 0:1].to_broadcast([1, 2]), op=OP.add))
        dve(lambda e: e.tensor_scalar(SMc(C_kb, 4, 0, 1)[0:1, 2:3], arv, 0.6,
                                      None, OP.mult))
        em.barrier()
        ar.reset(base)
        base = ar.save()
        w1 = [ar.bf16(1024) for _ in range(G)]
        w2 = [ar.bf16(512) for _ in range(G)]
        w3 = [ar.bf16(256) for _ in range(4)]
        w4 = [ar.bf16(1) for _ in range(2)]
        wstg_d2 = [ar.f32(1024) for _ in range(2)]
        xlf2 = [[ar.bf16(SLAB) for _ in range(G)] for _ in range(2)]
        obf = [[ar.bf16(SLAB) for _ in range(G)] for _ in range(2)]
        Dl = [ar.f32(SLAB) for _ in range(G)]
        xsl_d = [ar.f32(SLAB) for _ in range(G)]
        h1 = [ar.bf16(SLAB) for _ in range(G)]
        h2 = [ar.bf16(SLAB) for _ in range(4)]
        h3 = [ar.bf16(SLAB) for _ in range(2)]
        tfrow = ar.f32(SLAB, parts=1)
        crow = ar.f32(SLAB, parts=1)
        chi = ar.bf16(SLAB, parts=1)
        clo = ar.bf16(SLAB, parts=1)
        bCb = ar.bf16(SLAB)
        bCnb = ar.bf16(SLAB)
        tmbl = [ar.bf16(SLAB) for _ in range(G)]
        tmpf = [ar.f32(SLAB) for _ in range(G)]
        bA_d = ar.f32(SLAB)
        witems = ([(w1[g], cn_w1[g * 128:(g + 1) * 128, :], 1024)
                   for g in range(G)] +
                  [(w2[g], cn_w2[g * 128:(g + 1) * 128, :], 512)
                   for g in range(G)] +
                  [(w3[k], cn_w3[k * 128:(k + 1) * 128, :], 256)
                   for k in range(4)] +
                  [(w4[k], cn_w4[k * 128:(k + 1) * 128, :], 1)
                   for k in range(2)])
        stage_weights(wstg_d2, witems)
        em.barrier()
        twist_src = [4, 5, 6, 7, 0, 1, 2, 3]
        twist_neg = [False, False, True, True, True, True, False, False]

        def fd_load(s, rdm=None):
            if rdm is not None:
                em.wait("sync", "vector", rdm[0])
                em.wait("sync", "tensor", rdm[1])
            cl = s * SLAB
            for g in range(G):
                dma_in(xlf2[s % 2][g], h_xlf[g * 128:(g + 1) * 128, cl:cl + SLAB])
            return em.mark("sync")

        fd_ld = {0: fd_load(0)}
        fd_rdm = {}
        for s in range(NSLAB):
            c0 = s * SLAB
            b = s // SPB
            xlf = xlf2[s % 2]
            if s + 1 < NSLAB:
                fd_ld[s + 1] = fd_load(s + 1, fd_rdm.get(s - 1))
            if s > 0:
                em.wait("sync", "vector", fd_rdm[s - 1][0])
                em.wait("sync", "gpdma", fd_rdm[s - 1][2])
            for g in range(G):
                dma_in(xsl_d[g], h_x[g * 128:(g + 1) * 128, c0:c0 + SLAB])
            xs_ld = em.mark("sync")
            for dst in ("vector", "gpsimd", "tensor"):
                em.wait(dst, "sync", fd_ld[s])
            if s > 0:
                em.wait("gpsimd", "vector", fd_rdm[s - 1][0])
            for g in range(G):
                gp(lambda e, g=g: e.memset(Dl[g][:], 0.0))
            em.fence("gpsimd", "vector")
            cur = xlf
            for cyc in range(3):
                def evac1(fb, pst):
                    act(lambda e, fb=fb, pst=pst: e.activation(
                        h1[fb], pst[:], AF.Gelu,
                        bias=SMc(C_cnb1, 8)[:, fb:fb + 1]))
                proj_mm(G, G, lambda dg, fb: w1[dg][:, fb * 128:(fb + 1) * 128],
                        lambda dg, cur=cur: cur[dg], evac1)
                em.fence("scalar", "tensor")

                def evac2(fb, pst):
                    act(lambda e, fb=fb, pst=pst: e.activation(
                        h2[fb], pst[:], AF.Gelu,
                        bias=SMc(C_cnb2, 4)[:, fb:fb + 1]))
                proj_mm(4, G, lambda kg, fb: w2[kg][:, fb * 128:(fb + 1) * 128],
                        lambda kg: h1[kg], evac2)
                em.fence("scalar", "tensor")

                def evac3(fb, pst):
                    act(lambda e, fb=fb, pst=pst: e.activation(
                        h3[fb], pst[:], AF.Gelu,
                        bias=SMc(C_cnb3, 2)[:, fb:fb + 1]))
                proj_mm(2, 4, lambda kg, fb: w3[kg][:, fb * 128:(fb + 1) * 128],
                        lambda kg: h2[kg], evac3)
                em.fence("scalar", "tensor")
                for k in range(2):
                    pe(lambda e, k=k: e.matmul(PS[2][0:1, :], w4[k], h3[k],
                                               start=(k == 0), stop=(k == 1)))
                em.fence("tensor", "scalar")
                act(lambda e: e.activation(tfrow, PS[2][0:1, :], AF.Sigmoid,
                                           bias=SMc(C_sc3, 3, 0, 1)[0:1, 0:1]))
                em.fence("scalar", "vector")
                dve(lambda e, b=b: e.scalar_tensor_tensor(
                    crow, tfrow, SMc(C_kb, 4, 0, 1)[0:1, 2:3],
                    SMc(C_kb, 4, 0, 1)[0:1, b:b + 1].to_broadcast([1, SLAB]),
                    OP.mult, OP.add))
                em.sfence("vector")
                dve(lambda e: e.tensor_copy(chi, crow))
                em.sfence("vector")
                dve(lambda e: e.tensor_tensor(clo, crow, chi, op=OP.subtract))
                em.fence("vector", "tensor")
                bcast_rows(PS[3], chi, clo, 0, n=SLAB)
                bcast_rows(PS[4], chi, clo, 0, n=SLAB, neg=True)
                em.fence("tensor", "scalar")
                act(lambda e: e.copy(bCb, PS[3][:]))
                act(lambda e: e.copy(bCnb, PS[4][:]))
                em.fence("scalar", "vector")
                nxt = obf[cyc % 2]
                for g in range(G):
                    src = twist_src[g]
                    cb = bCnb if twist_neg[g] else bCb
                    dve(lambda e, g=g, src=src, cb=cb, cur=cur: e.tensor_tensor(
                        tmbl[g], cur[src], cb, op=OP.mult))
                for g in range(G):
                    dve(lambda e, g=g: e.tensor_tensor(Dl[g], Dl[g], tmbl[g],
                                                       op=OP.add))
                if cyc < 2:
                    for g in range(G):
                        dve(lambda e, g=g, nxt=nxt, xlf=xlf: e.tensor_tensor(
                            nxt[g], xlf[g], Dl[g], op=OP.add))
                em.fence("vector", "tensor")
                cur = nxt
            em.wait("vector", "sync", xs_ld)
            em.fence("scalar", "tensor")
            bcast_rows(PS[5], rows_a["Ahi"], rows_a["Alo"], c0)
            bcast_rows(PS[6], rows_a["Bhi"], rows_a["Blo"], c0)
            em.fence("tensor", "scalar")
            act(lambda e: e.copy(bA_d, PS[5][:]))
            em.fence("scalar", "vector")
            for g in range(G):
                dve(lambda e, g=g: e.tensor_tensor(tmpf[g], xsl_d[g], bA_d,
                                                   op=OP.mult))
            for g in range(G):
                dve(lambda e, g=g: e.tensor_tensor(xsl_d[g], xsl_d[g], tmpf[g],
                                                   op=OP.add))
            for g in range(G):
                em.sfence("vector")
                dve(lambda e, g=g: e.tensor_tensor(xsl_d[g], xsl_d[g], PS[6][:],
                                                   op=OP.add))
            for g in range(G):
                em.sfence("vector")
                dve(lambda e, g=g: e.tensor_tensor(xsl_d[g], xsl_d[g], Dl[g],
                                                   op=OP.add))
            for g in range(G):
                em.fence("vector", "gpsimd")
                dma_out(dst_hbm[g * 128:(g + 1) * 128, c0:c0 + SLAB], xsl_d[g])
            fd_rdm[s] = (em.mark("vector"), em.mark("tensor"),
                         em.mark("gpdma"))
        em.barrier()
        ar.reset(base)

    def pass_attn_b():
        ln_stats_pass(lambda g, c0: h_x[g * 128:(g + 1) * 128, c0:c0 + SLAB],
                      rows_a)
        base = ar.save()
        Wi = [ar.bf16(1024) for _ in range(G)]
        Wf = [ar.bf16(1024) for _ in range(G)]
        ppos_bb = ar.save()
        xsl2b = [[ar.f32(SLAB) for _ in range(G)] for _ in range(2)]
        wstg2b = [ARN[:, ppos_bb:ppos_bb + 1024],
                  ARN[:, ppos_bb + 1024:ppos_bb + 2048]]
        xlb2r = [[ar.bf16(SLAB) for _ in range(G)] for _ in range(2)]
        tmp = [ar.f32(SLAB) for _ in range(G)]
        bAb = ar.f32(SLAB)
        bBb = ar.f32(SLAB)
        isil = [ARN[:, dtsb_pos + g * SLAB:dtsb_pos + (g + 1) * SLAB]
                for g in range(G)]
        hscr = [[ar.f32(SLAB) for _ in range(G)] for _ in range(2)]
        stage_weights(wstg2b,
                      [(Wi[g], wt_i[g * 128:(g + 1) * 128, :], 1024)
                       for g in range(G)] +
                      [(Wf[g], wt_f[g * 128:(g + 1) * 128, :], 1024)
                       for g in range(G)])
        em.barrier()
        def bb_load(s, vm=None):
            if vm is not None:
                em.wait("sync", "vector", vm)
            cl = s * SLAB
            for g in range(G):
                dma_in(xsl2b[s % 2][g],
                       h_x[g * 128:(g + 1) * 128, cl:cl + SLAB])
            return em.mark("sync")

        bb_ld = {0: bb_load(0)}
        bb_rdm = {}
        for s in range(NSLAB):
            c0 = s * SLAB
            seq_start = (s % SPB == 0)
            if s + 1 < NSLAB:
                bb_ld[s + 1] = bb_load(s + 1,
                                       bb_rdm[s - 1][0] if s > 0 else None)
            em.wait("vector", "sync", bb_ld[s])
            if s >= 2:
                em.wait("vector", "tensor", bb_rdm[s - 2][1])
                em.wait("vector", "gpdma", bb_rdm[s - 2][2])
            if s >= 1:
                em.wait("scalar", "vector", bb_rdm[s - 1][0])
            xsl = xsl2b[s % 2]
            xlb = xlb2r[s % 2]
            fsig = xsl
            ln_apply_emit(xsl, xlb, rows_a, c0, tmp, bAb, bBb)
            em.fence("vector", ["gpsimd", "tensor"])
            for g in range(G):
                dma_out(h_xlb[g * 128:(g + 1) * 128, c0:c0 + SLAB], xlb[g])

            if s > 0:
                em.wait("scalar", "vector", bb_scan_m)

            def evac_i(fb, pst):
                act(lambda e, fb=fb, pst=pst: e.activation(isil[fb], pst[:],
                                                           AF.Silu))
            proj_mm(G, G, lambda dg, fb: Wi[dg][:, fb * 128:(fb + 1) * 128],
                    lambda dg, xlb=xlb: xlb[dg], evac_i)

            def evac_f(fb, pst, fsig=fsig):
                act(lambda e, fb=fb, pst=pst, fsig=fsig: e.activation(
                    fsig[fb], pst[:], AF.Sigmoid))
            proj_mm(G, G, lambda dg, fb: Wf[dg][:, fb * 128:(fb + 1) * 128],
                    lambda dg, xlb=xlb: xlb[dg], evac_f)
            em.fence("scalar", "vector")
            for g in range(G):
                dve(lambda e, g=g, fsig=fsig: e.tensor_scalar(
                    tmp[g], fsig[g], -1.0, 1.0, OP.mult, OP.add))
            for g in range(G):
                dve(lambda e, g=g: e.tensor_tensor(isil[g], isil[g], tmp[g],
                                                   op=OP.mult))
            if s >= 2:
                em.wait("vector", "gpdma", bb_rdm[s - 2][2])
            for g in range(G):
                if seq_start:
                    dve(lambda e, g=g, s=s, fsig=fsig: e.tensor_tensor_scan(
                        hscr[s % 2][g], fsig[g], isil[g], 0.0, OP.mult,
                        OP.add))
                else:
                    dve(lambda e, g=g, s=s, fsig=fsig: e.tensor_tensor_scan(
                        hscr[s % 2][g], fsig[g], isil[g],
                        SMc(C_carry, 8)[:, g:g + 1], OP.mult, OP.add))
            em.sfence("vector")
            for g in range(G):
                dve(lambda e, g=g, s=s: e.tensor_copy(
                    SMc(C_carry, 8)[:, g:g + 1],
                    hscr[s % 2][g][:, SLAB - 1:SLAB]))
                em.fence("vector", "gpsimd")
                dma_out(h_hb[g * 128:(g + 1) * 128, c0:c0 + SLAB],
                        hscr[s % 2][g])
            bb_scan_m = em.mark("vector")
            bb_rdm[s] = (bb_scan_m, em.mark("tensor"), em.mark("gpdma"))
        em.barrier()
        ar.reset(base)
        base = ar.save()
        Wg = [ar.bf16(1024) for _ in range(G)]
        Wo = [ar.bf16(1024) for _ in range(G)]
        wstg_o2 = [ar.f32(1024) for _ in range(2)]
        xlb2w = [[ar.bf16(SLAB) for _ in range(G)] for _ in range(2)]
        gg = [ar.f32(SLAB) for _ in range(G)]
        hbtw = [[ar.f32(SLAB) for _ in range(G)],
                [ARN[:, dtsb_pos + g * SLAB:dtsb_pos + (g + 1) * SLAB]
                 for g in range(G)]]
        ob = [ar.bf16(SLAB) for _ in range(G)]
        xsl_o = [ar.f32(SLAB) for _ in range(G)]
        sqb = [ar.bf16(SLAB) for _ in range(2)]
        bR = ar.f32(SLAB)
        rrow = ar.f32(SLAB, parts=1)
        rhi = ar.bf16(SLAB, parts=1)
        rlo = ar.bf16(SLAB, parts=1)
        tmpf2 = [ar.f32(SLAB) for _ in range(G)]
        stage_weights(wstg_o2,
                      [(Wg[g], wt_g[g * 128:(g + 1) * 128, :], 1024)
                       for g in range(G)] +
                      [(Wo[g], wt_o[g * 128:(g + 1) * 128, :], 1024)
                       for g in range(G)])
        em.barrier()
        def bc_load(s, rdm=None):
            if rdm is not None:
                em.wait("sync", "tensor", rdm[1])
                em.wait("sync", "scalar", rdm[2])
                em.wait("sync", "vector", rdm[0])
            cl = s * SLAB
            for g in range(G):
                dma_in(xlb2w[s % 2][g],
                       h_xlb[g * 128:(g + 1) * 128, cl:cl + SLAB])
                dma_in(hbtw[s % 2][g],
                       h_hb[g * 128:(g + 1) * 128, cl:cl + SLAB])
            return em.mark("sync")

        bc_ld = {0: bc_load(0)}
        bc_rdm = {}
        for s in range(NSLAB):
            c0 = s * SLAB
            xlb2 = xlb2w[s % 2]
            hbt = hbtw[s % 2]
            if s + 1 < NSLAB:
                bc_ld[s + 1] = bc_load(s + 1, bc_rdm.get(s - 1))
            if s > 0:
                em.wait("sync", "vector", bc_rdm[s - 1][0])
                em.wait("sync", "gpdma", bc_rdm[s - 1][3])
            for g in range(G):
                dma_in(xsl_o[g], h_x[g * 128:(g + 1) * 128, c0:c0 + SLAB])
            xo_ld = em.mark("sync")
            for dst in ("tensor", "scalar", "vector"):
                em.wait(dst, "sync", bc_ld[s])
            if s > 0:
                em.wait("scalar", "vector", bc_rdm[s - 1][0])

            def evac_g(fb, pst):
                act(lambda e, fb=fb, pst=pst: e.activation(gg[fb], pst[:],
                                                           AF.Silu))
            proj_mm(G, G, lambda dg, fb: Wg[dg][:, fb * 128:(fb + 1) * 128],
                    lambda dg, xlb2=xlb2: xlb2[dg], evac_g)
            for g in range(G):
                if g >= 2:
                    em.wait("scalar", "tensor")
                act(lambda e, g=g, hbt=hbt: e.activation(sqb[g % 2], hbt[g],
                                                         AF.Square))
                em.fence("scalar", "tensor")
                pe(lambda e, g=g: e.matmul(PS[2][0:1, :], ONESC[:, 2:3],
                                           sqb[g % 2], start=(g == 0),
                                           stop=(g == G - 1)))
            em.fence("tensor", "scalar")
            act(lambda e: e.copy(rrow, PS[2][0:1, :]))
            em.fence("scalar", "vector")
            dve(lambda e: e.tensor_scalar(rrow, rrow, 1.0 / D, None, OP.mult))
            em.fence("vector", "scalar")
            act(lambda e: e.activation(rrow, rrow, AF.Sqrt, bias=SMALL[0:1, C_eps:C_eps + 1]))
            em.fence("scalar", "vector")
            dve(lambda e: e.reciprocal(rrow, rrow))
            em.sfence("vector")
            dve(lambda e: e.tensor_copy(rhi, rrow))
            em.sfence("vector")
            dve(lambda e: e.tensor_tensor(rlo, rrow, rhi, op=OP.subtract))
            em.fence("vector", "tensor")
            em.fence("scalar", "tensor")
            bcast_rows(PS[4], rhi, rlo, 0, n=SLAB)
            em.fence("tensor", "scalar")
            act(lambda e: e.copy(bR, PS[4][:]))
            em.fence("scalar", "vector")
            for g in range(G):
                dve(lambda e, g=g, hbt=hbt: e.scalar_tensor_tensor(
                    tmpf2[g], hbt[g], SMc(C_bnw, 8)[:, g:g + 1], bR, OP.mult,
                    OP.mult))
            if s > 0:
                em.wait("vector", "tensor", bc_rdm[s - 1][1])
            for g in range(G):
                dve(lambda e, g=g: e.tensor_tensor(ob[g], tmpf2[g], gg[g],
                                                   op=OP.mult))
            em.fence("vector", "tensor")
            for dg in range(G):
                pst = PS[dg % 2]
                if dg >= 2:
                    em.wait("tensor", "vector")
                for kg in range(G):
                    pe(lambda e, dg=dg, kg=kg, pst=pst: e.matmul(
                        pst[:], Wo[kg][:, dg * 128:(dg + 1) * 128], ob[kg],
                        start=(kg == 0), stop=(kg == G - 1)))
                em.fence("tensor", "vector")
                if dg == 0:
                    em.wait("vector", "sync", xo_ld)
                dve(lambda e, dg=dg, pst=pst: e.tensor_tensor(
                    xsl_o[dg], xsl_o[dg], pst[:], op=OP.add))
                em.fence("vector", "gpsimd")
                dma_out(h_x[dg * 128:(dg + 1) * 128, c0:c0 + SLAB], xsl_o[dg])
            bc_rdm[s] = (em.mark("vector"), em.mark("tensor"),
                         em.mark("scalar"), em.mark("gpdma"))
        em.barrier()
        ar.reset(base)

    ps_on = PASSES
    if "stats" in ps_on:
        pass_a_stats()
    if "AB" in ps_on:
        pass_a_AB()
    if "Z" in ps_on:
        pass_a_Z()
    if "C1" in ps_on:
        pass_a_C1()
    if "C2" in ps_on:
        pass_a_C2()
    if nsub >= 2 and "F" in ps_on:
        pass_ffn(last=False)
    if nsub >= 3 and "B" in ps_on:
        pass_attn_b()
    if nsub >= 4 and "F" in ps_on:
        pass_ffn(last=True)
    if nsub < 4:
        base = ar.save()
        t = ar.f32(SLAB)
        for s in range(NSLAB):
            for g in range(G):
                dma_in(t, h_x[g * 128:(g + 1) * 128, s * SLAB:(s + 1) * SLAB])
                em.fence("sync", "gpsimd")
                dma_out(out_fm[g * 128:(g + 1) * 128, s * SLAB:(s + 1) * SLAB], t)
                em.fence("gpdma", "sync")
        ar.reset(base)
    em.barrier()

    with nc.Block() as block:
        em.replay(block)
    es.close()
    return nc


def shard_inputs(inputs, core):
    x = np.asarray(inputs["L0_attn_a_x"])
    xs = x[2 * core:2 * core + 2]
    x_fm = np.ascontiguousarray(xs.transpose(2, 0, 1).reshape(D, TC))
    d = {
        "x_fm": x_fm,
        "wt_in": np.ascontiguousarray(np.asarray(inputs["L0_attn_a_W_in"]).T),
        "wt_out": np.ascontiguousarray(np.asarray(inputs["L0_attn_a_W_out"]).T),
        "wt_i": np.ascontiguousarray(np.asarray(inputs["L0_attn_b_Wi"]).T),
        "wt_f": np.ascontiguousarray(np.asarray(inputs["L0_attn_b_Wf"]).T),
        "wt_g": np.ascontiguousarray(np.asarray(inputs["L0_attn_b_Wg"]).T),
        "wt_o": np.ascontiguousarray(np.asarray(inputs["L0_attn_b_Wo"]).T),
        "cn_w1": np.ascontiguousarray(np.asarray(inputs["L0_ffn_cn_w1"])),
        "cn_w2": np.ascontiguousarray(np.asarray(inputs["L0_ffn_cn_w2"])),
        "cn_w3": np.ascontiguousarray(np.asarray(inputs["L0_ffn_cn_w3"])),
        "cn_w4": np.ascontiguousarray(np.asarray(inputs["L0_ffn_cn_w4"])),
        "gc_w1": np.ascontiguousarray(np.asarray(inputs["L0_ffn_gc_w1"])),
        "gc_w2": np.ascontiguousarray(np.asarray(inputs["L0_ffn_gc_w2"])),
        "gc_w3": np.ascontiguousarray(np.asarray(inputs["L0_ffn_gc_w3"])),
        "p_convw": np.ascontiguousarray(
            np.asarray(inputs["L0_attn_a_conv_w"]).reshape(GC, 128, 4)
            .transpose(1, 0, 2).reshape(128, 72)),
        "p_convb": np.ascontiguousarray(
            np.asarray(inputs["L0_attn_a_conv_b"]).reshape(GC, 128).T),
        "p_ndtb": np.ascontiguousarray(
            -np.asarray(inputs["L0_attn_a_dt_bias"]).reshape(NH, 1)),
        "p_alog": np.ascontiguousarray(
            np.asarray(inputs["L0_attn_a_A_log"]).reshape(NH, 1)),
        "p_dskrow": np.ascontiguousarray(
            np.repeat(np.asarray(inputs["L0_attn_a_Dskip"]), HD).reshape(1, DI)),
        "p_anw": np.ascontiguousarray(
            np.asarray(inputs["L0_attn_a_norm_w"]).reshape(GI, 128).T),
        "p_lng": np.ascontiguousarray(
            np.asarray(inputs["L0_attn_a_ln_g"]).reshape(G, 128).T),
        "p_lnb": np.ascontiguousarray(
            np.asarray(inputs["L0_attn_a_ln_b"]).reshape(G, 128).T),
        "p_bnw": np.ascontiguousarray(
            np.asarray(inputs["L0_attn_b_norm_w"]).reshape(G, 128).T),
        "p_cnb1": np.ascontiguousarray(
            np.asarray(inputs["L0_ffn_cn_b1"]).reshape(G, 128).T),
        "p_cnb2": np.ascontiguousarray(
            np.asarray(inputs["L0_ffn_cn_b2"]).reshape(4, 128).T),
        "p_cnb3": np.ascontiguousarray(
            np.asarray(inputs["L0_ffn_cn_b3"]).reshape(2, 128).T),
        "p_gcb1": np.ascontiguousarray(
            np.asarray(inputs["L0_ffn_gc_b1"]).reshape(4, 128).T),
        "p_gcb2": np.ascontiguousarray(
            np.asarray(inputs["L0_ffn_gc_b2"]).reshape(2, 128).T),
        "p_sc3": np.ascontiguousarray(np.stack(
            [np.asarray(inputs["L0_ffn_cn_b4"]).reshape(()),
             np.asarray(inputs["L0_ffn_gc_b3"]).reshape(()),
             np.asarray(inputs["L0_ffn_adaptive_range"]).reshape(())]
        ).reshape(1, 3).astype(np.float32)),
    }
    return {k: np.ascontiguousarray(np.asarray(v, dtype=np.float32))
            for k, v in d.items()}


_NC_CACHE = {}


def kernel(**inputs):
    nsub = 4
    if nsub not in _NC_CACHE:
        _NC_CACHE[nsub] = build_program(nsub)
    nc = _NC_CACHE[nsub]
    in_maps = [shard_inputs(inputs, c) for c in range(8)]
    res = run_bass_kernel_spmd(nc, in_maps, core_ids=list(range(8)))
    out = np.empty((16, 2048, 1024), dtype=np.float32)
    for c in range(8):
        o = np.asarray(res.results[c]["out_fm"])
        out[2 * c:2 * c + 2] = o.reshape(D, NB, T).transpose(1, 2, 0)
    return out
